# version 04 (final)
# baseline (speedup 1.0000x reference)
import contextlib
import numpy as np
import ml_dtypes
import concourse.bass as bass
import concourse.mybir as mybir
from concourse.bass_utils import run_bass_kernel_spmd

F32 = mybir.dt.float32
BF16 = mybir.dt.bfloat16
AF = mybir.ActivationFunctionType
ALU = mybir.AluOpType
AX = mybir.AxisListType
bf16 = ml_dtypes.bfloat16

NCORES = 8
B, L, D = 2, 8192, 2048
T = B * L
TPC = T // NCORES
NTT = TPC // 128
NDC = D // 128
EPS = 1e-5

ENGS = ("sync", "vector", "scalar", "tensor", "gpsimd")


class Prog:

    def __init__(self, nc):
        self.nc = nc
        self.items = {e: [] for e in ENGS}
        self.counts = {}
        self.sem_names = []

    def sem(self, name):
        if name not in self.counts:
            self.counts[name] = 0
            self.sem_names.append(name)
        return name

    def n(self, name):
        return self.counts.get(name, 0)

    def _add(self, eng, fn, waits, inc, k):
        if inc is not None:
            self.sem(inc)
            self.counts[inc] += k
        for s, _ in (waits or []):
            self.sem(s)
        self.items[eng].append((fn, list(waits or []), inc, k))
        return self.counts[inc] if inc is not None else None

    def op(self, eng, fn, wait=None, sem=None, k=1):
        return self._add(eng, fn, wait, sem, k)

    def dma(self, eng, out, in_, wait=None, sem=None):
        return self._add(eng, lambda e, s, out=out, in_=in_: e.dma_start(out=out, in_=in_),
                         wait, sem, 16)

    def wait(self, eng, waits):
        self._add(eng, None, waits, None, 0)

    def barrier(self):
        ws = [(s, self.n(s)) for s in self.sem_names]
        for e in ENGS:
            self.wait(e, ws)

    def finalize(self):
        nc = self.nc
        with contextlib.ExitStack() as stack:
            sems = {nm: stack.enter_context(nc.semaphore(nm)) for nm in self.sem_names}
            block = stack.enter_context(nc.Block())

            def mk(eng_name):
                def body(engobj):
                    for fn, waits, inc, k in self.items[eng_name]:
                        for s, v in waits:
                            engobj.wait_ge(sems[s], v)
                        if fn is not None:
                            instr = fn(engobj, sems)
                            if inc is not None:
                                instr.then_inc(sems[inc], k)
                return body

            for eng_name in ENGS:
                if self.items[eng_name]:
                    getattr(block, eng_name)(mk(eng_name))
        return nc


class _NcProxy:

    def __init__(self, nc, pfx):
        self._nc = nc
        self._pfx = pfx

    def sbuf_tensor(self, *a, **kw):
        if a and isinstance(a[0], str):
            a = (self._pfx + a[0],) + a[1:]
        return self._nc.sbuf_tensor(*a, **kw)

    def psum_tensor(self, *a, **kw):
        if a and isinstance(a[0], str):
            a = (self._pfx + a[0],) + a[1:]
        return self._nc.psum_tensor(*a, **kw)

    def __getattr__(self, k):
        return getattr(self._nc, k)


def _mega_ctx(mega, pfx=""):
    if mega is None:
        nc = bass.Bass("TRN2", target_bir_lowering=False)
        p = Prog(nc)
        reg = {}
    else:
        nc, p, reg = mega
        nc = _NcProxy(nc, pfx)
        p.barrier()

    def dram(name, shape, dt, kind=None, addr_space=None):
        if name in reg:
            return reg[name]
        kw = {}
        if kind:
            kw["kind"] = kind
        if addr_space:
            kw["addr_space"] = addr_space
        t = nc.dram_tensor(name, shape, dt, **kw)
        reg[name] = t
        return t

    return nc, p, dram, mega is None



def emit_ln_tile(p, x_tile, xn_tile, scr, vsquare=False):
    W = x_tile.shape[1]
    sq, r1, r2, inv, nbias = scr["sq"], scr["r1"], scr["r2"], scr["inv"], scr["nbias"]
    mean, m2, var, lnv = scr["mean"], scr["m2"], scr["var"], scr["lnv"]
    if vsquare:
        p.op("vector", lambda e, s: e.tensor_mul(sq[:, 0:W], x_tile, x_tile),
             wait=[("v", p.n("v")), ("a", p.n("a"))], sem="v")
        a_sq = p.n("a")
        p.op("vector", lambda e, s: e.reduce_sum(r1[:], x_tile, AX.X), sem="v")
        v1 = p.n("v")
        p.op("vector", lambda e, s: e.tensor_reduce(r2[:], sq[:, 0:W], AX.X, ALU.add),
             wait=[("v", p.n("v"))], sem="v")
        v2 = p.n("v")
    else:
        p.op("scalar", lambda e, s: e.activation(sq[:, 0:W], x_tile, AF.Square),
             wait=[("v", p.n("v"))], sem="a")
        a_sq = p.n("a")
        p.op("vector", lambda e, s: e.reduce_sum(r1[:], x_tile, AX.X), sem="v")
        v1 = p.n("v")
        p.op("vector", lambda e, s: e.tensor_reduce(r2[:], sq[:, 0:W], AX.X, ALU.add),
             wait=[("a", a_sq)], sem="v")
        v2 = p.n("v")
    p.op("vector", lambda e, s: e.tensor_scalar_mul(mean[:], r1[:], 1.0 / W),
         wait=[("v", v1)], sem="v")
    v3 = p.n("v")
    p.op("vector", lambda e, s: e.tensor_scalar(r2[:], r2[:], 1.0 / W, EPS, ALU.mult, ALU.add),
         wait=[("v", v2)], sem="v")
    v4 = p.n("v")
    p.op("vector", lambda e, s: e.tensor_mul(m2[:], mean[:], mean[:]),
         wait=[("v", v3)], sem="v")
    v5 = p.n("v")
    p.op("vector", lambda e, s: e.scalar_tensor_tensor(
        var[:], m2[:], -1.0, r2[:], ALU.mult, ALU.add),
        wait=[("v", v5)], sem="v")
    v6 = p.n("v")
    p.op("scalar", lambda e, s: e.activation(lnv[:], var[:], AF.Ln),
         wait=[("v", v6)], sem="a")
    a_ln = p.n("a")
    p.op("scalar", lambda e, s: e.activation(inv[:], lnv[:], AF.Exp, scale=-0.5),
         wait=[("a", a_ln)], sem="a")
    a2 = p.n("a")
    p.op("vector", lambda e, s: e.tensor_scalar(
        nbias[:], mean[:], inv[:, 0:1], -1.0, ALU.mult, ALU.mult),
        wait=[("a", a2)], sem="v")
    v7 = p.n("v")
    p.op("vector", lambda e, s: e.scalar_tensor_tensor(
        xn_tile, x_tile, inv[:, 0:1], nbias[:, 0:1].broadcast_to(tuple(xn_tile.shape)),
        ALU.mult, ALU.add), wait=[("v", v7)], sem="v")
    return p.n("v")


def emit_ln_transpose_stage(p, E, nc, x_h, xn_fm, ident, row_tiles=NTT, keep_x=None):
    x_t = E(nc.sbuf_tensor([128, D], F32))
    sq = E(nc.sbuf_tensor([128, D], F32))
    r1 = E(nc.sbuf_tensor([128, 1], F32))
    r2 = E(nc.sbuf_tensor([128, 1], F32))
    inv = E(nc.sbuf_tensor([128, 1], F32))
    nbias = E(nc.sbuf_tensor([128, 1], F32))
    xn_tm = E(nc.sbuf_tensor([128, D], BF16))
    mean = E(nc.sbuf_tensor([128, 1], F32))
    m2 = E(nc.sbuf_tensor([128, 1], F32))
    var = E(nc.sbuf_tensor([128, 1], F32))
    lnv = E(nc.sbuf_tensor([128, 1], F32))
    ptr = E(nc.psum_tensor([128, 128], BF16))
    scr = {"sq": sq, "r1": r1, "r2": r2, "inv": inv, "nbias": nbias,
           "mean": mean, "m2": m2, "var": var, "lnv": lnv}

    for tt in range(row_tiles):
        dst = x_t[:] if keep_x is None else keep_x[:, tt * D:(tt + 1) * D]
        src_tile = x_t if keep_x is None else None
        ldw = p.dma("sync", dst, x_h[tt * 128:(tt + 1) * 128, :],
                    wait=[("free", p.n("free"))] if keep_x is None else [], sem="ld")
        xin = dst
        p.wait("vector", [("ld", ldw)])
        p.wait("scalar", [("ld", ldw)])
        emit_ln_tile(p, xin, xn_tm[:], scr)
        for j in range(NDC):
            vw = p.n("v")
            p.op("tensor", lambda e, s, j=j: e.transpose(
                ptr[:], xn_tm[:, j * 128:(j + 1) * 128], ident[:]),
                wait=[("v", vw)], sem="pe")
            pw = p.n("pe")
            p.op("vector", lambda e, s, tt=tt, j=j: e.tensor_copy(
                xn_fm[:, j * TPC + tt * 128: j * TPC + (tt + 1) * 128], ptr[:]),
                wait=[("pe", pw)], sem="v")
        if keep_x is None:
            p.op("vector", lambda e, s: e.engine_nop(), sem="free", k=16)


HFF = 5120
NHC = HFF // 128


def build_l1c(upto="full", mega=None):
    nc, p, dram, standalone = _mega_ctx(mega, "pf_")
    x_h = dram("x" if standalone else "x3", [TPC, D], F32, kind="ExternalInput")
    wfcT_h = dram("wfcT", [D, HFF], F32, kind="ExternalInput")
    wprojT_h = dram("wprojT", [HFF, D], F32, kind="ExternalInput")
    ident_h = dram("ident", [128, 128], BF16, kind="ExternalInput")
    out_h = dram("out", [TPC, D], F32, kind="ExternalOutput")
    wfc_t = dram("wfc_t", [D, HFF], BF16, kind="ExternalOutput")
    wproj_t = dram("wproj_t", [HFF, D], BF16, kind="ExternalOutput")

    for s in ("ld", "v", "a", "pe", "st", "wld", "free", "pebuf", "ype"):
        p.sem(s)

    with contextlib.ExitStack() as stack:
        E = stack.enter_context
        ident = E(nc.sbuf_tensor([128, 128], BF16))
        xn_fm = E(nc.sbuf_tensor([128, NDC * TPC], BF16))
        dbg_h = nc.dram_tensor("dbg", [128, NDC * TPC], BF16, kind="ExternalOutput") \
            if upto in ("ln", "tern") else (
            nc.dram_tensor("dbg", [2, 128, 1], F32, kind="ExternalOutput")
            if upto in ("stats0", "stats") else None)
        p.dma("sync", ident[:], ident_h[:], sem="ld")
        with contextlib.ExitStack() as ln_stack:
            emit_ln_transpose_stage(p, ln_stack.enter_context, nc, x_h, xn_fm, ident)
        if upto == "ln":
            p.dma("sync", dbg_h[:], xn_fm[:], wait=[("v", p.n("v"))], sem="st")
            p.wait("sync", [("st", p.n("st"))])
            p.finalize()
            return nc

        acc1 = E(nc.sbuf_tensor([128, 1], F32))
        accp = E(nc.sbuf_tensor([128, 1], F32))
        redt = E(nc.sbuf_tensor([128, 1], F32))
        ones_col = E(nc.sbuf_tensor([128, 1], F32))
        ones_row1 = E(nc.sbuf_tensor([1, 128], F32))
        sc_ps = E(nc.psum_tensor([1, 1], F32))
        bc_ps = E(nc.psum_tensor([128, 1], F32))
        thr1 = E(nc.sbuf_tensor([128, 1], F32))
        thr2 = E(nc.sbuf_tensor([128, 1], F32))
        nthr = E(nc.sbuf_tensor([128, 1], F32))
        se = E(nc.sbuf_tensor([128, 1], F32))
        s1c = E(nc.sbuf_tensor([128, 1], F32))
        s2c = E(nc.sbuf_tensor([128, 1], F32))
        tern_stack = contextlib.ExitStack()
        Et = tern_stack.enter_context
        wstream = Et(nc.sbuf_tensor([128, HFF], F32))
        wsgn_a = Et(nc.sbuf_tensor([128, HFF], F32))
        wsgn_b = Et(nc.sbuf_tensor([128, HFF], F32))
        wtern = Et(nc.sbuf_tensor([128, HFF], BF16))

        p.op("vector", lambda e, s: e.memset(ones_col[:], 1.0))
        p.op("vector", lambda e, s: e.memset(ones_row1[:], 1.0))
        p.op("vector", lambda e, s: e.memset(acc1[:], 0.0))
        p.op("vector", lambda e, s: e.memset(accp[:], 0.0), sem="v", k=1)
        gp_ready = p.n("v")
        p.wait("tensor", [("v", gp_ready)])

        def tern_stats(w_h_, nchunks, width, acc):
            for c in range(nchunks):
                ldw = p.dma("sync", wstream[:, 0:width], w_h_[c * 128:(c + 1) * 128, :],
                            wait=[("v", p.n("v"))], sem="ld")
                p.op("vector", lambda e, s, width=width: e.tensor_reduce(
                    redt[:], wstream[:, 0:width], AX.X, ALU.add, apply_absolute_value=True),
                    wait=[("ld", ldw)], sem="v")
                p.op("vector", lambda e, s, acc=acc: e.tensor_add(acc[:], acc[:], redt[:]),
                     wait=[("v", p.n("v"))], sem="v")

        def tern_scalar(acc, nelem, thr_col, s_col):
            vw = p.n("v")
            p.op("tensor", lambda e, s, acc=acc: e.matmul(
                sc_ps[:], ones_col[:], acc[:], start=True, stop=True),
                wait=[("v", vw)], sem="pe")
            pw = p.n("pe")
            p.op("vector", lambda e, s, nelem=nelem: e.tensor_scalar_mul(
                redt[0:1, :], sc_ps[:], 1.0 / nelem), wait=[("pe", pw)], sem="v")
            p.op("vector", lambda e, s: e.tensor_scalar_max(redt[0:1, :], redt[0:1, :], 1e-5),
                 wait=[("v", p.n("v"))], sem="v")
            vw = p.n("v")
            p.op("tensor", lambda e, s: e.matmul(
                bc_ps[:], ones_row1[:], redt[0:1, :], start=True, stop=True),
                wait=[("v", vw)], sem="pe")
            pw = p.n("pe")
            p.op("vector", lambda e, s, s_col=s_col: e.tensor_copy(s_col[:], bc_ps[:]),
                 wait=[("pe", pw)], sem="v")
            p.op("vector", lambda e, s, s_col=s_col, thr_col=thr_col: e.tensor_scalar_mul(
                thr_col[:], s_col[:], 0.7), wait=[("v", p.n("v"))], sem="v")

        def tern_write(w_h_, scratch, nchunks, width, thr_col):
            p.op("vector", lambda e, s, thr_col=thr_col: e.tensor_scalar_mul(
                nthr[:], thr_col[:], -1.0), wait=[("a", p.n("a"))], sem="v")
            for c in range(nchunks):
                ldw = p.dma("sync", wstream[:, 0:width], w_h_[c * 128:(c + 1) * 128, :],
                            wait=[("v", p.n("v")), ("st", p.n("st"))], sem="ld")
                vready = p.n("v")
                p.op("scalar", lambda e, s, width=width: e.activation(
                    wsgn_a[:, 0:width], wstream[:, 0:width], AF.Sign, bias=nthr[:, 0:1]),
                    wait=[("ld", ldw), ("v", vready)])
                p.op("scalar", lambda e, s, width=width, thr_col=thr_col: e.activation(
                    wsgn_b[:, 0:width], wstream[:, 0:width], AF.Sign, bias=thr_col[:, 0:1]),
                    sem="a")
                aw = p.n("a")
                p.op("vector", lambda e, s, width=width: e.tensor_add(
                    wtern[:, 0:width], wsgn_a[:, 0:width], wsgn_b[:, 0:width]),
                    wait=[("a", aw)], sem="v")
                p.dma("sync", scratch[c * 128:(c + 1) * 128, :], wtern[:, 0:width],
                      wait=[("v", p.n("v"))], sem="st")

        tern_stats(wfcT_h, NDC, HFF, acc1)
        if upto == "stats0":
            p.dma("sync", dbg_h[0], acc1[:], wait=[("v", p.n("v"))], sem="st")
            p.wait("sync", [("st", p.n("st"))])
            tern_stack.close()
            p.finalize()
            return nc
        tern_stats(wprojT_h, NHC, D, accp)
        tern_scalar(acc1, D * HFF, thr1, s1c)
        tern_scalar(accp, D * HFF, thr2, s2c)
        if upto == "stats":
            p.dma("sync", dbg_h[0], thr1[:], wait=[("v", p.n("v"))], sem="st")
            p.dma("sync", dbg_h[1], thr2[:], wait=[("v", p.n("v"))], sem="st")
            p.wait("sync", [("st", p.n("st"))])
            tern_stack.close()
            p.finalize()
            return nc
        p.op("vector", lambda e, s: e.tensor_mul(se[:], s1c[:], s1c[:]), sem="v")
        p.op("vector", lambda e, s: e.tensor_mul(se[:], se[:], s2c[:]),
             wait=[("v", p.n("v"))], sem="v")
        p.op("vector", lambda e, s: e.tensor_scalar_mul(se[:], se[:], 0.125),
             wait=[("v", p.n("v"))], sem="v")
        tern_write(wfcT_h, wfc_t, NDC, HFF, thr1)
        tern_write(wprojT_h, wproj_t, NHC, D, thr2)
        tern_done = p.n("st")
        tern_stack.close()
        if upto == "tern":
            p.dma("sync", dbg_h[:], xn_fm[:], wait=[("v", p.n("v")), ("st", tern_done)], sem="st")
            p.wait("sync", [("st", p.n("st"))])
            p.finalize()
            return nc

        wtile = [E(nc.sbuf_tensor(f"wtile{i}", [128, 256], BF16)) for i in range(2)]
        h_fm = E(nc.sbuf_tensor([128, NHC * 512], BF16))
        hrelu = E(nc.sbuf_tensor([128, 512], F32))
        wprow = [E(nc.sbuf_tensor(f"wprow{i}", [128, 512], BF16)) for i in range(NHC)]
        x_ts = [E(nc.sbuf_tensor(f"x_ts{i}", [128, D], F32)) for i in range(4)]
        o_st = [E(nc.sbuf_tensor(f"o_st{i}", [128, 512], F32)) for i in range(2)]
        hp = [E(nc.psum_tensor(f"hp{i}", [128, 512], F32)) for i in range(2)]
        yp = [E(nc.psum_tensor(f"yp{i}", [128, 512], F32)) for i in range(2)]

        for tc in range(4):
            st_prev = p.n("st")
            for hg in range(NHC // 2):
                base_pebuf = p.n("pebuf")
                for k in range(NDC):
                    buf = wtile[k % 2]
                    if k < 2:
                        w_ld = [("st", tern_done), ("pe", p.n("pe"))]
                    else:
                        w_ld = [("st", tern_done), ("pebuf", base_pebuf + k - 1)]
                    ldw = p.dma("sync", buf[:],
                                wfc_t[k * 128:(k + 1) * 128, hg * 256:(hg + 1) * 256],
                                wait=w_ld, sem="ld")
                    for j in range(2):
                        hc = hg * 2 + j
                        w_ = [("ld", ldw)]
                        if k == 0:
                            w_.append(("a", p.n("a")))
                        last = (k == NDC - 1)
                        semname = "pe" if (last and j == 1) else (
                            "pebuf" if (j == 1 and not last) else None)
                        p.op("tensor", lambda e, s, buf=buf, j=j, hc=hc, k=k, tc=tc: e.matmul(
                            hp[hc % 2][:], buf[:, j * 128:(j + 1) * 128],
                            xn_fm[:, k * TPC + tc * 512:k * TPC + (tc + 1) * 512],
                            start=(k == 0), stop=(k == NDC - 1)),
                            wait=w_, sem=semname)
                pw = p.n("pe")
                for j in range(2):
                    hc = hg * 2 + j
                    p.op("scalar", lambda e, s, hc=hc: e.activation(
                        hrelu[:], hp[hc % 2][:], AF.Relu), wait=[("pe", pw)], sem="a")
                    p.op("scalar", lambda e, s, hc=hc: e.activation(
                        h_fm[:, hc * 512:(hc + 1) * 512], hrelu[:], AF.Square),
                        wait=[("a", p.n("a"))], sem="a")
            h_done = p.n("a")
            ldx = []
            for ts in range(4):
                tt = tc * 4 + ts
                ldx.append(p.dma("sync", x_ts[ts][:], x_h[tt * 128:(tt + 1) * 128, :],
                                 wait=[("st", p.n("st"))], sem="ld"))
            for dc in range(4):
                wld_base = p.n("wld")
                for k2 in range(NHC):
                    p.dma("sync", wprow[k2][:],
                          wproj_t[k2 * 128:(k2 + 1) * 128, dc * 512:(dc + 1) * 512],
                          wait=[("st", tern_done), ("ype", p.n("ype"))], sem="wld")
                wld_need = p.n("wld")
                for ts in range(4):
                    for k2 in range(NHC):
                        w_ = []
                        if k2 == 0:
                            w_ = [("wld", wld_need), ("a", h_done), ("v", p.n("v"))]
                        p.op("tensor", lambda e, s, k2=k2, dc=dc, ts=ts: e.matmul(
                            yp[ts % 2][:],
                            h_fm[:, k2 * 512 + ts * 128: k2 * 512 + (ts + 1) * 128],
                            wprow[k2][:],
                            start=(k2 == 0), stop=(k2 == NHC - 1)),
                            wait=w_,
                            sem=("ype" if k2 == NHC - 1 else None))
                    pw = p.n("ype")
                    ob = o_st[(dc * 4 + ts) % 2]
                    p.op("vector", lambda e, s, dc=dc, ts=ts, ob=ob: e.scalar_tensor_tensor(
                        ob[:], yp[ts % 2][:], se[:, 0:1],
                        x_ts[ts][:, dc * 512:(dc + 1) * 512], ALU.mult, ALU.add),
                        wait=[("ype", pw), ("ld", ldx[ts]),
                              ("st", max(st_prev, p.n("st") - 16))], sem="v")
                    tt = tc * 4 + ts
                    p.dma("sync", out_h[tt * 128:(tt + 1) * 128, dc * 512:(dc + 1) * 512],
                          ob[:], wait=[("v", p.n("v"))], sem="st")

        p.wait("sync", [("st", p.n("st"))])
        if standalone:
            p.finalize()
    return nc





def build_l0a(dbg_xn=False, dbg_tile0=False, mega=None):
    nc, p, dram, standalone = _mega_ctx(mega, "pa_")
    x_h = dram("x", [TPC, D], F32, kind="ExternalInput")
    wdbcT_h = dram("wdbcT", [D, 160], F32, kind="ExternalInput")
    bdbc_lo_h = dram("bdbc_lo", [128, 1], F32, kind="ExternalInput")
    bdbc_bc_h = dram("bdbc_bc", [32, 1], F32, kind="ExternalInput")
    wdtT_h = dram("wdtT", [128, D], F32, kind="ExternalInput")
    bdt_h = dram("bdt", [128, NDC], F32, kind="ExternalInput")
    dvec_h = dram("dvec", [1, D], F32, kind="ExternalInput")
    ident_h = dram("ident", [128, 128], BF16, kind="ExternalInput")
    ones_row_h = dram("ones_row", [1, 128], BF16, kind="ExternalInput")
    xr_h = dram("xr", [TPC, D], F32, kind="ExternalOutput")
    bc_h = dram("bc", [32, TPC], F32, kind="ExternalOutput")
    delta_h = dram("delta", [D, TPC], F32, kind="ExternalOutput")
    u_h = dram("u", [D, TPC], F32, kind="ExternalOutput")

    if dbg_tile0:
        dbg2_h = dram("dbg2", [6, 128, 2048], F32, kind="ExternalOutput")
    for s in ("ld", "gld", "v", "a", "pe", "st", "free", "pebuf"):
        p.sem(s)

    with contextlib.ExitStack() as stack:
        E = stack.enter_context
        ident = E(nc.sbuf_tensor([128, 128], BF16))
        ones_row_b = E(nc.sbuf_tensor([1, 128], BF16))
        xn_fm = E(nc.sbuf_tensor([128, NDC * TPC], BF16))
        db_t = E(nc.sbuf_tensor([128, D], F32))
        dvec_sb = E(nc.sbuf_tensor([1, D], F32))
        bps = E(nc.psum_tensor([128, 512], F32))

        p.dma("sync", ident[:], ident_h[:], sem="ld")
        p.dma("sync", ones_row_b[:], ones_row_h[:], sem="ld")
        p.dma("sync", dvec_sb[:], dvec_h[:], sem="ld")
        ld0 = p.n("ld")
        ones_row_f = E(nc.sbuf_tensor([1, 128], F32))
        p.op("vector", lambda e, s: e.memset(ones_row_f[:], 1.0), sem="v")

        vw0 = p.n("v")
        for c in range(4):
            p.op("tensor", lambda e, s, c=c: e.matmul(
                bps[:], ones_row_f[:], dvec_sb[:, c * 512:(c + 1) * 512],
                start=True, stop=True),
                wait=[("ld", ld0), ("v", vw0 if c == 0 else p.n("v"))], sem="pe")
            pw = p.n("pe")
            p.op("vector", lambda e, s, c=c: e.tensor_copy(
                db_t[:, c * 512:(c + 1) * 512], bps[:]), wait=[("pe", pw)], sem="v")

        x_t = E(nc.sbuf_tensor([128, D], F32))
        mcol_o = E(nc.sbuf_tensor([128, 1], F32))
        mrow_o = E(nc.sbuf_tensor([1, 128], F32))
        r2i_o = E(nc.sbuf_tensor([128, 1], F32))
        with contextlib.ExitStack() as lns:
            El = lns.enter_context
            sq = El(nc.sbuf_tensor([128, D], F32))
            xn_f32 = El(nc.sbuf_tensor([128, D], F32))
            xr_t = El(nc.sbuf_tensor([128, D], F32))
            xn_tm = El(nc.sbuf_tensor([128, D], BF16))
            r1 = El(nc.sbuf_tensor([128, 1], F32))
            r2 = El(nc.sbuf_tensor([128, 1], F32))
            inv = El(nc.sbuf_tensor([128, 1], F32))
            nbias = El(nc.sbuf_tensor([128, 1], F32))
            mean = El(nc.sbuf_tensor([128, 1], F32))
            m2 = El(nc.sbuf_tensor([128, 1], F32))
            var = El(nc.sbuf_tensor([128, 1], F32))
            lnv = El(nc.sbuf_tensor([128, 1], F32))
            ptr = El(nc.psum_tensor([128, 128], BF16))
            scr = {"sq": sq, "r1": r1, "r2": r2, "inv": inv, "nbias": nbias,
                   "mean": mean, "m2": m2, "var": var, "lnv": lnv}
            for tt in range(NTT):
                ldw = p.dma("sync", x_t[:], x_h[tt * 128:(tt + 1) * 128, :],
                            wait=[("free", p.n("free"))], sem="ld")
                p.wait("vector", [("ld", ldw), ("a", p.n("a"))])
                p.wait("scalar", [("ld", ldw)])
                emit_ln_tile(p, x_t[:], xn_f32[:], scr)
                if dbg_tile0 and tt == 0:
                    vnow = p.n("v")
                    p.dma("sync", dbg2_h[0], x_t[:], wait=[("v", vnow)], sem="st")
                    p.dma("sync", dbg2_h[1], sq[:], wait=[("v", vnow)], sem="st")
                    for di, tcol in enumerate((r1, r2, mean, m2, var, inv)):
                        p.op("vector", lambda e, s, di=di, tcol=tcol: e.tensor_copy(
                            xr_t[:, di:di + 1], tcol[:]), wait=[("v", vnow)], sem="v")
                    p.dma("sync", dbg2_h[2], xr_t[:], wait=[("v", p.n("v"))], sem="st")
                    p.dma("sync", dbg2_h[3], xn_f32[:], wait=[("v", vnow)], sem="st")
                vx = p.n("v")
                p.op("scalar", lambda e, s: e.activation(xn_tm[:], xn_f32[:], AF.Identity),
                     wait=[("v", vx)], sem="a")
                a_bf = p.n("a")
                p.op("vector", lambda e, s: e.tensor_mul(xr_t[:], xn_f32[:], db_t[:]),
                     wait=[("v", vx)], sem="v")
                p.op("vector", lambda e, s: e.tensor_add(xr_t[:], xr_t[:], x_t[:]),
                     wait=[("v", p.n("v"))], sem="v")
                p.dma("sync", xr_h[tt * 128:(tt + 1) * 128, :],
                      xn_f32[:] if dbg_xn else xr_t[:],
                      wait=[("v", p.n("v"))], sem="st")
                p.wait("tensor", [("a", a_bf)])
                for j in range(NDC):
                    vw = p.n("v")
                    p.op("tensor", lambda e, s, j=j: e.transpose(
                        ptr[:], xn_tm[:, j * 128:(j + 1) * 128], ident[:]),
                        wait=[("v", vw)], sem="pe")
                    pw = p.n("pe")
                    p.op("vector", lambda e, s, tt=tt, j=j: e.tensor_copy(
                        xn_fm[:, j * TPC + tt * 128: j * TPC + (tt + 1) * 128], ptr[:]),
                        wait=[("pe", pw)], sem="v")
                p.op("vector", lambda e, s: e.engine_nop(),
                     wait=[("st", p.n("st"))], sem="free", k=16)

        wdbc_sb = E(nc.sbuf_tensor([128, NDC * 160], BF16))
        blo = E(nc.sbuf_tensor([128, 1], F32))
        bbc = E(nc.sbuf_tensor([32, 1], F32))
        dlo_sb = E(nc.sbuf_tensor([128, TPC], BF16))
        bc_sb = E(nc.sbuf_tensor([32, TPC], F32))
        psA = E(nc.psum_tensor([128, 512], F32))
        psB = E(nc.psum_tensor([32, 512], F32))
        wdbc_f32 = E(nc.sbuf_tensor([128, 160], F32))
        for k in range(NDC):
            aw = p.n("a")
            ldk = p.dma("sync", wdbc_f32[:], wdbcT_h[k * 128:(k + 1) * 128, :],
                        wait=[("a", aw)], sem="ld")
            p.op("scalar", lambda e, s, k=k: e.activation(
                wdbc_sb[:, k * 160:(k + 1) * 160], wdbc_f32[:], AF.Identity),
                wait=[("ld", ldk)], sem="a")
        wdbc_acast = p.n("a")
        p.dma("sync", blo[:], bdbc_lo_h[:], sem="ld")
        p.dma("sync", bbc[:], bdbc_bc_h[:], sem="ld")
        wdbc_ld = p.n("ld")
        xn_ready = p.n("v")
        for tc in range(4):
            for k in range(NDC):
                w_ = [("ld", wdbc_ld), ("a", wdbc_acast), ("v", xn_ready)]
                if k == 0:
                    w_.append(("a", p.n("a")))
                p.op("tensor", lambda e, s, k=k, tc=tc: e.matmul(
                    psA[:], wdbc_sb[:, k * 160:k * 160 + 128],
                    xn_fm[:, k * TPC + tc * 512:k * TPC + (tc + 1) * 512],
                    start=(k == 0), stop=(k == NDC - 1)),
                    wait=w_, sem=("pe" if k == NDC - 1 else None))
                p.op("tensor", lambda e, s, k=k, tc=tc: e.matmul(
                    psB[:], wdbc_sb[:, k * 160 + 128:k * 160 + 160],
                    xn_fm[:, k * TPC + tc * 512:k * TPC + (tc + 1) * 512],
                    start=(k == 0), stop=(k == NDC - 1)),
                    sem=("pe" if k == NDC - 1 else None))
            pw = p.n("pe")
            p.op("scalar", lambda e, s, tc=tc: e.activation(
                dlo_sb[:, tc * 512:(tc + 1) * 512], psA[:], AF.Identity,
                bias=blo[:, 0:1]), wait=[("pe", pw)], sem="a")
            p.op("scalar", lambda e, s, tc=tc: e.activation(
                bc_sb[:, tc * 512:(tc + 1) * 512], psB[:], AF.Identity,
                bias=bbc[:, 0:1]), sem="a")
        afin = p.n("a")
        for c in range(4):
            p.dma("sync", bc_h[:, c * 512:(c + 1) * 512], bc_sb[:, c * 512:(c + 1) * 512],
                  wait=[("a", afin)], sem="st")

        wdt_sb = E(nc.sbuf_tensor([128, D], BF16))
        bdt_sb = E(nc.sbuf_tensor([128, NDC], F32))
        t0 = E(nc.sbuf_tensor([128, 512], F32))
        t1 = E(nc.sbuf_tensor([128, 512], F32))
        trelu = E(nc.sbuf_tensor([128, 512], F32))
        dl_t = [E(nc.sbuf_tensor(f"dl{i}", [128, 512], F32)) for i in range(2)]
        u_t = [E(nc.sbuf_tensor(f"ut{i}", [128, 512], F32)) for i in range(2)]
        xnf = E(nc.sbuf_tensor([128, 512], F32))
        psD = [E(nc.psum_tensor(f"psD{i}", [128, 512], F32)) for i in range(2)]
        wdt_f32 = E(nc.sbuf_tensor([128, D], F32))
        ldwdt = p.dma("sync", wdt_f32[:], wdtT_h[:], sem="ld")
        p.op("scalar", lambda e, s: e.activation(wdt_sb[:], wdt_f32[:], AF.Identity),
             wait=[("ld", ldwdt)], sem="a")
        wdt_acast = p.n("a")
        p.dma("sync", bdt_sb[:], bdt_h[:], sem="ld")
        wdt_ld = p.n("ld")
        for m in range(NDC):
            for tc in range(4):
                i = (m * 4 + tc) % 2
                pw_need = p.n("a") if (m * 4 + tc) >= 2 else afin
                p.op("tensor", lambda e, s, m=m, tc=tc, i=i: e.matmul(
                    psD[i][:], wdt_sb[:, m * 128:(m + 1) * 128],
                    dlo_sb[:, tc * 512:(tc + 1) * 512],
                    start=True, stop=True),
                    wait=[("ld", wdt_ld), ("a", max(pw_need, wdt_acast))],
                    sem="pe")
                pw = p.n("pe")
                p.op("scalar", lambda e, s, m=m, i=i: e.activation(
                    t0[:], psD[i][:], AF.Identity, bias=bdt_sb[:, m:m + 1]),
                    wait=[("pe", pw), ("v", p.n("v"))], sem="a")
                p.op("scalar", lambda e, s: e.activation(t1[:], t0[:], AF.Abs),
                     wait=[("a", p.n("a"))], sem="a")
                p.op("scalar", lambda e, s: e.activation(t1[:], t1[:], AF.Exp, scale=-1.0),
                     wait=[("a", p.n("a"))], sem="a")
                p.op("scalar", lambda e, s: e.activation(t1[:], t1[:], AF.Ln, bias=1.0),
                     wait=[("a", p.n("a"))], sem="a")
                a_sp = p.n("a")
                p.op("vector", lambda e, s: e.tensor_scalar_max(trelu[:], t0[:], 0.0),
                     wait=[("a", a_sp)], sem="v")
                p.op("vector", lambda e, s, i=i: e.tensor_add(dl_t[i][:], trelu[:], t1[:]),
                     wait=[("v", p.n("v"))], sem="v")
                vd = p.n("v")
                p.op("scalar", lambda e, s, m=m, tc=tc: e.activation(
                    xnf[:], xn_fm[:, m * TPC + tc * 512:m * TPC + (tc + 1) * 512],
                    AF.Identity), wait=[("v", p.n("v"))], sem="a")
                axn = p.n("a")
                p.op("vector", lambda e, s, i=i: e.tensor_mul(u_t[i][:], dl_t[i][:], xnf[:]),
                     wait=[("a", axn), ("v", vd)], sem="v")
                vu = p.n("v")
                p.dma("sync", delta_h[m * 128:(m + 1) * 128, tc * 512:(tc + 1) * 512],
                      dl_t[i][:], wait=[("v", vd)], sem="st")
                p.dma("sync", u_h[m * 128:(m + 1) * 128, tc * 512:(tc + 1) * 512],
                      u_t[i][:], wait=[("v", vu)], sem="st")
        p.wait("sync", [("st", p.n("st"))])
        if standalone:
            p.finalize()
    return nc



NSG = 16


def build_l0b(mega=None):
    nc, p, dram, standalone = _mega_ctx(mega, "pb_")
    delta_h = dram("delta", [D, TPC], F32, kind="ExternalInput")
    u_h = dram("u", [D, TPC], F32, kind="ExternalInput")
    bc_h = dram("bc", [32, TPC], F32, kind="ExternalInput")
    alog_h = dram("alog", [D, NSG], F32, kind="ExternalInput")
    mask_h = dram("mask", [128, 8], F32, kind="ExternalInput")
    imask_h = dram("imask", [128, 8], F32, kind="ExternalInput")
    selb_h = dram("selb", [32, 32 * 128], BF16, kind="ExternalInput")
    y_h = dram("y", [D, TPC], F32, kind="ExternalOutput")
    states_loc = dram("states_loc", [D, 32], F32)
    states_ag = dram("states_ag", [8 * D, 32], F32, addr_space="Shared")

    for s_ in ("ld", "v", "a", "pe", "st", "cc", "sst"):
        p.sem(s_)

    with contextlib.ExitStack() as stack:
        E = stack.enter_context
        A_sb = E(nc.sbuf_tensor([128, NDC * NSG], F32))
        bc_bf = E(nc.sbuf_tensor([32, TPC], BF16))
        selb = E(nc.sbuf_tensor([32, 32 * 128], BF16))
        mask = E(nc.sbuf_tensor([128, 8], F32))
        imask = E(nc.sbuf_tensor([128, 8], F32))
        Bb = E(nc.sbuf_tensor([128, NSG * 512], F32))
        Cb = E(nc.sbuf_tensor([128, NSG * 512], F32))
        dl = [E(nc.sbuf_tensor(f"dl{i}", [128, 512], F32)) for i in range(2)]
        uu = [E(nc.sbuf_tensor(f"uu{i}", [128, 512], F32)) for i in range(2)]
        at = E(nc.sbuf_tensor([128, 512], F32))
        aexp = E(nc.sbuf_tensor([128, 512], F32))
        bx = E(nc.sbuf_tensor([128, 512], F32))
        hbuf = [E(nc.sbuf_tensor(f"hb{i}", [128, 512], F32)) for i in range(2)]
        zbuf = E(nc.sbuf_tensor([128, 512], F32))
        ybuf = [E(nc.sbuf_tensor(f"yb{i}", [128, 512], F32)) for i in range(2)]
        hstate = E(nc.sbuf_tensor([128, NDC * NSG], F32))
        ttot = E(nc.sbuf_tensor([128, NDC], F32))
        red = E(nc.sbuf_tensor([128, 1], F32))
        ptot = E(nc.sbuf_tensor([128, NSG], F32))
        stmp = E(nc.sbuf_tensor([128, NSG], F32))
        stmp2 = E(nc.sbuf_tensor([128, NSG], F32))
        bcs = E(nc.sbuf_tensor([32, TPC], F32))
        psB = [E(nc.psum_tensor(f"psB{i}", [128, 512], F32)) for i in range(2)]

        ld_a = []
        atile = E(nc.sbuf_tensor([128, NSG], F32))
        for g in range(NDC):
            lw = p.dma("sync", atile[:], alog_h[g * 128:(g + 1) * 128, :],
                       wait=[("a", p.n("a"))], sem="ld")
            p.op("scalar", lambda e, s, g=g: e.activation(
                A_sb[:, g * NSG:(g + 1) * NSG], atile[:], AF.Exp),
                wait=[("ld", lw)], sem="a")
        aw = p.n("a")
        p.op("vector", lambda e, s: e.tensor_scalar_mul(A_sb[:], A_sb[:], -1.0),
             wait=[("a", aw)], sem="v")
        lw = p.dma("sync", bcs[:], bc_h[:], sem="ld")
        p.op("scalar", lambda e, s: e.activation(bc_bf[:], bcs[:], AF.Identity),
             wait=[("ld", lw)], sem="a")
        p.dma("sync", selb[:], selb_h[:], sem="ld")
        p.dma("sync", mask[:], mask_h[:], sem="ld")
        p.dma("sync", imask[:], imask_h[:], sem="ld")
        prep_ld = p.n("ld")
        prep_a = p.n("a")
        p.op("vector", lambda e, s: e.memset(hstate[:], 0.0))
        p.op("vector", lambda e, s: e.memset(ttot[:], 0.0), sem="v")
        p.wait("tensor", [("a", prep_a), ("ld", prep_ld)])

        def build_bcast(dst, row_base, tc, with_c):
            n = NSG * (2 if with_c else 1)
            for i in range(n):
                srow = row_base + (i if i < NSG else 16 + (i - NSG))
                tgt = dst if i < NSG else Cb
                col = (i % NSG) * 512
                ps = psB[i % 2]
                vw = p.n("v")
                p.op("tensor", lambda e, s, srow=srow, tc=tc, ps=ps: e.matmul(
                    ps[:], selb[:, srow * 128:(srow + 1) * 128],
                    bc_bf[:, tc * 512:(tc + 1) * 512], start=True, stop=True),
                    wait=[("v", vw)], sem="pe")
                pw = p.n("pe")
                p.op("vector", lambda e, s, tgt=tgt, col=col, ps=ps: e.tensor_copy(
                    tgt[:, col:col + 512], ps[:]), wait=[("pe", pw)], sem="v")

        def scan_pass(pass2):
            for tc in range(4):
                build_bcast(Bb, 0, tc, with_c=pass2)
                bcast_v = p.n("v")
                for g in range(NDC):
                    i = g % 2
                    vneed = p.n("v")
                    l1 = p.dma("sync", dl[i][:], delta_h[g * 128:(g + 1) * 128,
                                                         tc * 512:(tc + 1) * 512],
                               wait=[("v", vneed)], sem="ld")
                    l2 = p.dma("sync", uu[i][:], u_h[g * 128:(g + 1) * 128,
                                                     tc * 512:(tc + 1) * 512],
                               wait=[("v", vneed)], sem="ld")
                    p.wait("vector", [("ld", l2), ("v", bcast_v)])
                    if not pass2:
                        p.op("vector", lambda e, s, i=i: e.reduce_sum(
                            red[:], dl[i][:], AX.X), sem="v")
                        p.op("vector", lambda e, s, g=g: e.tensor_add(
                            ttot[:, g:g + 1], ttot[:, g:g + 1], red[:]),
                            wait=[("v", p.n("v"))], sem="v")
                    for s_i in range(NSG):
                        col = g * NSG + s_i
                        p.op("vector", lambda e, s, i=i, col=col: e.tensor_scalar(
                            at[:], dl[i][:], A_sb[:, col:col + 1], None, ALU.mult),
                            wait=[("a", p.n("a"))], sem="v")
                        vv = p.n("v")
                        p.op("scalar", lambda e, s: e.activation(
                            aexp[:], at[:], AF.Exp), wait=[("v", vv)], sem="a")
                        a_e = p.n("a")
                        p.op("vector", lambda e, s, i=i, s_i=s_i: e.tensor_mul(
                            bx[:], uu[i][:], Bb[:, s_i * 512:(s_i + 1) * 512]), sem="v")
                        vb = p.n("v")
                        hb = hbuf[s_i % 2]
                        p.op("vector", lambda e, s, hb=hb, col=col: e.tensor_tensor_scan(
                            hb[:], aexp[:], bx[:], hstate[:, col:col + 1],
                            ALU.mult, ALU.add),
                            wait=[("a", a_e), ("v", vb)], sem="v")
                        vs = p.n("v")
                        p.op("vector", lambda e, s, hb=hb, col=col: e.tensor_copy(
                            hstate[:, col:col + 1], hb[:, 511:512]),
                            wait=[("v", vs)], sem="v")
                        if pass2:
                            p.op("vector", lambda e, s, hb=hb, s_i=s_i: e.tensor_mul(
                                zbuf[:], hb[:], Cb[:, s_i * 512:(s_i + 1) * 512]),
                                wait=[("v", p.n("v"))], sem="v")
                            yb = ybuf[g % 2]
                            if s_i == 0:
                                p.op("vector", lambda e, s, yb=yb: e.tensor_copy(
                                    yb[:], zbuf[:]), wait=[("v", p.n("v")),
                                                           ("st", p.n("st"))], sem="v")
                            else:
                                p.op("vector", lambda e, s, yb=yb: e.tensor_add(
                                    yb[:], yb[:], zbuf[:]),
                                    wait=[("v", p.n("v"))], sem="v")
                    if pass2:
                        p.dma("sync", y_h[g * 128:(g + 1) * 128, tc * 512:(tc + 1) * 512],
                              ybuf[g % 2][:], wait=[("v", p.n("v"))], sem="st")

        scan_pass(False)
        for g in range(NDC):
            p.op("vector", lambda e, s, g=g: e.tensor_scalar(
                stmp[:], A_sb[:, g * NSG:(g + 1) * NSG], ttot[:, g:g + 1], None,
                ALU.mult), wait=[("v", p.n("v")), ("a", p.n("a"))], sem="v")
            vv = p.n("v")
            p.op("scalar", lambda e, s: e.activation(ptot[:], stmp[:], AF.Exp),
                 wait=[("v", vv)], sem="a")
            av = p.n("a")
            p.dma("sync", states_loc[g * 128:(g + 1) * 128, 0:16],
                  hstate[:, g * NSG:(g + 1) * NSG], wait=[("v", vv)], sem="sst")
            p.dma("sync", states_loc[g * 128:(g + 1) * 128, 16:32],
                  ptot[:], wait=[("a", av)], sem="sst")
        sstn = p.n("sst")
        p.wait("gpsimd", [("sst", sstn)])
        p.op("gpsimd", lambda e, s: e.collective_compute(
            "AllGather", ALU.bypass, replica_groups=[list(range(8))],
            ins=[states_loc[:]], outs=[states_ag[:]]), sem="cc")
        Hr = E(nc.sbuf_tensor([128, NSG], F32))
        Pr = E(nc.sbuf_tensor([128, NSG], F32))
        hin = E(nc.sbuf_tensor([128, NSG], F32))
        for g in range(NDC):
            p.op("vector", lambda e, s: e.memset(hin[:], 0.0),
                 wait=[("v", p.n("v"))], sem="v")
            for r in range(8):
                lw1 = p.dma("sync", Hr[:], states_ag[r * D + g * 128: r * D + (g + 1) * 128, 0:16],
                            wait=[("cc", 1), ("v", p.n("v"))], sem="ld")
                lw2 = p.dma("sync", Pr[:], states_ag[r * D + g * 128: r * D + (g + 1) * 128, 16:32],
                            wait=[("cc", 1)], sem="ld")
                p.op("vector", lambda e, s, r=r: e.scalar_tensor_tensor(
                    stmp[:], Pr[:], mask[:, r:r + 1],
                    imask[:, r:r + 1].broadcast_to((128, NSG)), ALU.mult, ALU.add),
                    wait=[("ld", lw2), ("v", p.n("v"))], sem="v")
                p.op("vector", lambda e, s, r=r: e.tensor_scalar(
                    stmp2[:], Hr[:], mask[:, r:r + 1], None, ALU.mult),
                    wait=[("v", p.n("v"))], sem="v")
                p.op("vector", lambda e, s: e.tensor_mul(hin[:], hin[:], stmp[:]),
                     wait=[("v", p.n("v"))], sem="v")
                p.op("vector", lambda e, s: e.tensor_add(hin[:], hin[:], stmp2[:]),
                     wait=[("v", p.n("v"))], sem="v")
            p.op("vector", lambda e, s, g=g: e.tensor_copy(
                hstate[:, g * NSG:(g + 1) * NSG], hin[:]),
                wait=[("v", p.n("v"))], sem="v")
        scan_pass(True)
        p.wait("sync", [("st", p.n("st"))])
        if standalone:
            p.finalize()
    return nc



NE = 16
IEX = 1408
NIC = IEX // 128
SH = 2816


def build_l0c(mega=None):
    nc, p, dram, standalone = _mega_ctx(mega, "pc_")
    xr_h = dram("xr", [TPC, D], F32, kind="ExternalInput")
    ymix_h = None if mega else dram("ymix", [TPC, D], F32, kind="ExternalInput")
    y_fm_h = dram("y", [D, TPC], F32, kind="ExternalOutput") if mega else None
    gwT_h = dram("gwT", [D, NE], F32, kind="ExternalInput")
    w1T_h = dram("w1T", [NE, D, IEX], F32, kind="ExternalInput")
    w3T_h = dram("w3T", [NE, D, IEX], F32, kind="ExternalInput")
    w2T_h = dram("w2T", [NE, IEX, D], F32, kind="ExternalInput")
    ws1T_h = dram("ws1T", [D, SH], F32, kind="ExternalInput")
    ws3T_h = dram("ws3T", [D, SH], F32, kind="ExternalInput")
    ws2T_h = dram("ws2T", [SH, D], F32, kind="ExternalInput")
    ident_h = dram("ident", [128, 128], BF16, kind="ExternalInput")
    identf_h = dram("identf", [128, 128], F32, kind="ExternalInput")
    sel16_h = dram("sel16", [16, 16 * 128], BF16, kind="ExternalInput")
    x2_h = dram("x2", [TPC, D], F32, kind="ExternalOutput")
    x1_dram = dram("x1_dram", [TPC, D], F32, kind="ExternalOutput")

    HT = 1024
    NT8 = HT // 128
    MGRPS = ((0, 4), (4, 4), (8, 3))

    for s_ in ("ld", "v", "a", "pe", "st"):
        p.sem(s_)

    with contextlib.ExitStack() as stack:
        E = stack.enter_context
        ident = E(nc.sbuf_tensor([128, 128], BF16))
        identf = E(nc.sbuf_tensor([128, 128], F32))
        sel16 = E(nc.sbuf_tensor([16, 16 * 128], BF16))
        x1n_fm = E(nc.sbuf_tensor([128, NDC * HT], BF16))
        y_acc = E(nc.sbuf_tensor([128, NT8 * D], F32))
        cfm = E(nc.sbuf_tensor([16, HT], F32))
        cbf = E(nc.sbuf_tensor([16, HT], BF16))
        gw_sb = E(nc.sbuf_tensor([128, NDC * NE], BF16))
        gwf = E(nc.sbuf_tensor([128, NE], F32))
        h1s = E(nc.sbuf_tensor([128, NIC * HT], BF16))
        h3s = E(nc.sbuf_tensor([128, 4 * HT], BF16))
        cbe = E(nc.sbuf_tensor([128, HT], BF16))
        wst = [E(nc.sbuf_tensor(f"wst{i}", [128, 512], F32)) for i in range(2)]
        wst_a = [0, 0]
        w_sb = E(nc.sbuf_tensor([128, NDC * 4 * 128], BF16))
        w2_sb = E(nc.sbuf_tensor([128, NIC * 512], BF16))
        x1t = E(nc.sbuf_tensor([128, D], F32))
        yt = E(nc.sbuf_tensor([128, D], F32))
        gps = E(nc.psum_tensor([16, 512], F32))
        ptg = E(nc.psum_tensor([128, 128], F32))
        cps = E(nc.psum_tensor([128, 512], F32))
        hps = [E(nc.psum_tensor(f"hps{i}", [128, 512], F32)) for i in range(2)]
        yps = [E(nc.psum_tensor(f"yps{i}", [128, 512], F32)) for i in range(2)]
        ptr = E(nc.psum_tensor([128, 128], BF16))
        stile = E(nc.sbuf_tensor([128, 16], F32))
        gsc = E(nc.sbuf_tensor([128, 4], F32))
        gs2 = E(nc.sbuf_tensor([128, 4], F32))
        mm1 = E(nc.sbuf_tensor([128, 1], F32))
        mm2 = E(nc.sbuf_tensor([128, 1], F32))
        eqt = E(nc.sbuf_tensor([128, 16], F32))
        msk = E(nc.sbuf_tensor([128, 16], F32))
        msk2 = E(nc.sbuf_tensor([128, 16], F32))
        wraw = E(nc.sbuf_tensor([128, 16], F32))
        den = E(nc.sbuf_tensor([128, 1], F32))

        p.dma("sync", ident[:], ident_h[:], sem="ld")
        p.dma("sync", identf[:], identf_h[:], sem="ld")
        p.dma("sync", sel16[:], sel16_h[:], sem="ld")
        for k in range(NDC):
            aw = p.n("a")
            lw = p.dma("sync", gwf[:], gwT_h[k * 128:(k + 1) * 128, :],
                       wait=[("a", aw)], sem="ld")
            p.op("scalar", lambda e, s, k=k: e.activation(
                gw_sb[:, k * NE:(k + 1) * NE], gwf[:], AF.Identity),
                wait=[("ld", lw)], sem="a")
        gw_cast = p.n("a")
        base_ld = p.n("ld")

        def vedge(fn, extra=None):
            w_ = [("v", p.n("v"))]
            if extra:
                w_ += extra
            p.op("vector", fn, wait=w_, sem="v")

        for hf in range(2):
            tok0 = hf * HT
            with contextlib.ExitStack() as lns:
                El = lns.enter_context
                sq = El(nc.sbuf_tensor([128, D], F32))
                xn_tm = El(nc.sbuf_tensor([128, D], BF16))
                r1 = El(nc.sbuf_tensor([128, 1], F32))
                r2 = El(nc.sbuf_tensor([128, 1], F32))
                inv = El(nc.sbuf_tensor([128, 1], F32))
                nbias = El(nc.sbuf_tensor([128, 1], F32))
                meanb = El(nc.sbuf_tensor([128, 1], F32))
                m2b = El(nc.sbuf_tensor([128, 1], F32))
                varb = El(nc.sbuf_tensor([128, 1], F32))
                lnvb = El(nc.sbuf_tensor([128, 1], F32))
                scr = {"sq": sq, "r1": r1, "r2": r2, "inv": inv, "nbias": nbias,
                       "mean": meanb, "m2": m2b, "var": varb, "lnv": lnvb}
                for tt in range(NT8):
                    row = tok0 + tt * 128
                    l1 = p.dma("sync", x1t[:], xr_h[row:row + 128, :],
                               wait=[("st", p.n("st")), ("v", p.n("v")),
                                     ("a", p.n("a"))], sem="ld")
                    if mega:
                        l2 = p.dma("sync",
                                   yt[:].rearrange("p (g c) -> p g c", c=128),
                                   y_fm_h[:, row:row + 128].rearrange(
                                       "(g p) c -> p g c", p=128),
                                   wait=[("pe", p.n("pe"))], sem="ld")
                        for g in range(NDC):
                            p.op("tensor", lambda e, s, g=g: e.transpose(
                                ptg[:], yt[:, g * 128:(g + 1) * 128], identf[:]),
                                wait=[("ld", max(l1, l2)), ("v", p.n("v"))],
                                sem="pe")
                            pw_y = p.n("pe")
                            p.op("vector", lambda e, s, g=g: e.tensor_add(
                                x1t[:, g * 128:(g + 1) * 128],
                                x1t[:, g * 128:(g + 1) * 128], ptg[:]),
                                wait=[("pe", pw_y), ("v", p.n("v"))], sem="v")
                    else:
                        l2 = p.dma("sync", yt[:], ymix_h[row:row + 128, :], sem="ld")
                        p.op("vector", lambda e, s: e.tensor_add(
                            x1t[:], x1t[:], yt[:]), wait=[("ld", l2)], sem="v")
                    p.dma("sync", x1_dram[row:row + 128, :], x1t[:],
                          wait=[("v", p.n("v"))], sem="st")
                    p.wait("scalar", [("v", p.n("v"))])
                    emit_ln_tile(p, x1t[:], xn_tm[:], scr)
                    for j in range(NDC):
                        vw = p.n("v")
                        p.op("tensor", lambda e, s, j=j: e.transpose(
                            ptr[:], xn_tm[:, j * 128:(j + 1) * 128], ident[:]),
                            wait=[("v", vw), ("ld", base_ld)], sem="pe")
                        pw = p.n("pe")
                        p.op("vector", lambda e, s, tt=tt, j=j: e.tensor_copy(
                            x1n_fm[:, j * HT + tt * 128: j * HT + (tt + 1) * 128],
                            ptr[:]), wait=[("pe", pw)], sem="v")
            xn_ready = p.n("v")

            for tc2 in range(2):
                for k in range(NDC):
                    p.op("tensor", lambda e, s, k=k, tc2=tc2: e.matmul(
                        gps[:], gw_sb[:, k * NE:(k + 1) * NE],
                        x1n_fm[:, k * HT + tc2 * 512:k * HT + (tc2 + 1) * 512],
                        start=(k == 0), stop=(k == NDC - 1)),
                        wait=([("v", xn_ready), ("a", max(gw_cast, p.n("a")))]
                              if k == 0 else []),
                        sem=("pe" if k == NDC - 1 else None))
                pw = p.n("pe")
                p.op("scalar", lambda e, s, tc2=tc2: e.activation(
                    cfm[:, tc2 * 512:(tc2 + 1) * 512], gps[:], AF.Sigmoid),
                    wait=[("pe", pw)], sem="a")
            sig_a = p.n("a")
            for tt in range(NT8):
                pos = tt * 128
                vw = p.n("v")
                p.op("tensor", lambda e, s, pos=pos: e.transpose(
                    ptg[:, 0:16], cfm[0:16, pos:pos + 128], identf[0:16, 0:16]),
                    wait=[("a", sig_a), ("v", vw)], sem="pe")
                pw = p.n("pe")
                vedge(lambda e, s: e.tensor_copy(stile[:], ptg[:, 0:16]), [("pe", pw)])
                vedge(lambda e, s: e.tensor_reduce(
                    gsc[:], stile[:].rearrange("p (a b) -> p a b", b=4), AX.X, ALU.max))
                vedge(lambda e, s: e.tensor_reduce(mm1[:], gsc[:], AX.X, ALU.max))
                vedge(lambda e, s: e.tensor_scalar(
                    eqt[:, 0:4], gsc[:], mm1[:, 0:1], None, ALU.is_ge))
                vedge(lambda e, s: e.scalar_tensor_tensor(
                    gs2[:], eqt[:, 0:4], -1e9, gsc[:], ALU.mult, ALU.add))
                vedge(lambda e, s: e.tensor_reduce(mm2[:], gs2[:], AX.X, ALU.max))
                vedge(lambda e, s: e.tensor_scalar(
                    gsc[:], gsc[:], mm2[:, 0:1], None, ALU.is_ge))
                vedge(lambda e, s: e.tensor_mul(
                    msk[:].rearrange("p (a b) -> p a b", b=4),
                    stile[:].rearrange("p (a b) -> p a b", b=4),
                    gsc[:].rearrange("p (a one) -> p a one", one=1).broadcast_to((128, 4, 4))))
                for it in range(3):
                    vedge(lambda e, s: e.tensor_reduce(mm1[:], msk[:], AX.X, ALU.max))
                    vedge(lambda e, s: e.tensor_scalar(
                        eqt[:], msk[:], mm1[:, 0:1], None, ALU.is_ge))
                    vedge(lambda e, s: e.scalar_tensor_tensor(
                        msk[:], eqt[:], -1e9, msk[:], ALU.mult, ALU.add))
                vedge(lambda e, s: e.tensor_reduce(mm2[:], msk[:], AX.X, ALU.max))
                vedge(lambda e, s: e.tensor_mul(
                    msk2[:].rearrange("p (a b) -> p a b", b=4),
                    stile[:].rearrange("p (a b) -> p a b", b=4),
                    gsc[:].rearrange("p (a one) -> p a one", one=1).broadcast_to((128, 4, 4))))
                vedge(lambda e, s: e.tensor_scalar(
                    eqt[:], msk2[:], mm2[:, 0:1], None, ALU.is_ge))
                vedge(lambda e, s: e.tensor_mul(wraw[:], stile[:], eqt[:]))
                vedge(lambda e, s: e.tensor_reduce(den[:], wraw[:], AX.X, ALU.add))
                vedge(lambda e, s: e.reciprocal(den[:], den[:]))
                vedge(lambda e, s: e.tensor_scalar(
                    wraw[:], wraw[:], den[:, 0:1], None, ALU.mult))
                vw = p.n("v")
                p.op("tensor", lambda e, s: e.transpose(
                    ptg[0:16, :], wraw[:], identf[:]), wait=[("v", vw)], sem="pe")
                pw = p.n("pe")
                vedge(lambda e, s, tt=tt: e.tensor_copy(
                    cfm[0:16, tt * 128:(tt + 1) * 128], ptg[0:16, 0:128]), [("pe", pw)])
            vw = p.n("v")
            p.op("scalar", lambda e, s: e.activation(cbf[:], cfm[:], AF.Identity),
                 wait=[("v", vw)], sem="a")
            comb_a = p.n("a")

            for tt in range(NT8):
                vedge(lambda e, s, tt=tt: e.memset(y_acc[:, tt * D:(tt + 1) * D], 0.0))
            yinit_v = p.n("v")

            items = [("e", e) for e in range(NE)] + [("s", 0), ("s", 1)]
            for kind, ei in items:
                if kind == "e":
                    for half in range(2):
                        vw = p.n("v")
                        p.op("tensor", lambda e, s, ei=ei, half=half: e.matmul(
                            cps[:], sel16[:, ei * 128:(ei + 1) * 128],
                            cbf[:, half * 512:(half + 1) * 512], start=True, stop=True),
                            wait=[("a", comb_a), ("v", vw)], sem="pe")
                        pw = p.n("pe")
                        vedge(lambda e, s, half=half: e.tensor_copy(
                            cbe[:, half * 512:(half + 1) * 512], cps[:]), [("pe", pw)])
                for m0, mcnt in MGRPS:
                    mw = mcnt * 128
                    for which, dst_act in ((1, AF.Silu), (3, None)):
                        for k in range(NDC):
                            i = k % 2
                            aw = wst_a[i]
                            if kind == "e":
                                hmat = w1T_h if which == 1 else w3T_h
                                src = hmat[ei, k * 128:(k + 1) * 128,
                                           m0 * 128:m0 * 128 + mw]
                            else:
                                hmat = ws1T_h if which == 1 else ws3T_h
                                src = hmat[k * 128:(k + 1) * 128,
                                           ei * IEX + m0 * 128:ei * IEX + m0 * 128 + mw]
                            lw = p.dma("sync", wst[i][:, 0:min(mw, 512)] if mw <= 512
                                       else wst[i][:, 0:512], src if mw <= 512 else None,
                                       wait=[("a", aw)], sem="ld") if mw <= 512 else None
                            if mw <= 512:
                                wst_a[i] = p.op(
                                    "scalar", lambda e, s, k=k, i=i, mw=mw: e.activation(
                                        w_sb[:, k * mw:(k + 1) * mw], wst[i][:, 0:mw],
                                        AF.Identity),
                                    wait=[("ld", lw), ("pe", p.n("pe"))], sem="a")
                            else:
                                for part in range(2):
                                    pwd = min(512, mw - part * 512)
                                    i2 = (k * 2 + part) % 2
                                    aw2 = wst_a[i2]
                                    if kind == "e":
                                        src = hmat[ei, k * 128:(k + 1) * 128,
                                                   m0 * 128 + part * 512:
                                                   m0 * 128 + part * 512 + pwd]
                                    else:
                                        src = hmat[k * 128:(k + 1) * 128,
                                                   ei * IEX + m0 * 128 + part * 512:
                                                   ei * IEX + m0 * 128 + part * 512 + pwd]
                                    lw2 = p.dma("sync", wst[i2][:, 0:pwd], src,
                                                wait=[("a", aw2)], sem="ld")
                                    wst_a[i2] = p.op(
                                        "scalar", lambda e, s, k=k, i2=i2, mw=mw,
                                        part=part, pwd=pwd: e.activation(
                                            w_sb[:, k * mw + part * 512:
                                                 k * mw + part * 512 + pwd],
                                            wst[i2][:, 0:pwd], AF.Identity),
                                        wait=[("ld", lw2), ("pe", p.n("pe"))], sem="a")
                        cast_a = p.n("a")
                        for mi in range(mcnt):
                            m = m0 + mi
                            for tc2 in range(2):
                                pi = (m * 2 + tc2) % 2
                                for k in range(NDC):
                                    w_ = []
                                    if k == 0:
                                        w_ = [("a", max(cast_a, p.n("a"))),
                                              ("v", max(yinit_v, p.n("v")))]
                                    p.op("tensor", lambda e, s, mi=mi, tc2=tc2, k=k,
                                         pi=pi, mw=mw: e.matmul(
                                        hps[pi][:],
                                        w_sb[:, k * mw + mi * 128:k * mw + (mi + 1) * 128],
                                        x1n_fm[:, k * HT + tc2 * 512:k * HT + (tc2 + 1) * 512],
                                        start=(k == 0), stop=(k == NDC - 1)),
                                        wait=w_, sem=("pe" if k == NDC - 1 else None))
                                pw = p.n("pe")
                                if which == 1:
                                    p.op("scalar", lambda e, s, m=m, tc2=tc2, pi=pi: e.activation(
                                        h1s[:, m * HT + tc2 * 512:m * HT + (tc2 + 1) * 512],
                                        hps[pi][:], AF.Silu),
                                        wait=[("pe", pw)], sem="a")
                                else:
                                    p.op("scalar", lambda e, s, mi=mi, tc2=tc2, pi=pi: e.activation(
                                        h3s[:, mi * HT + tc2 * 512:mi * HT + (tc2 + 1) * 512],
                                        hps[pi][:], AF.Identity),
                                        wait=[("pe", pw)], sem="a")
                    h_done = p.n("a")
                    for mi in range(mcnt):
                        m = m0 + mi
                        vedge(lambda e, s, m=m, mi=mi: e.tensor_mul(
                            h1s[:, m * HT:(m + 1) * HT], h1s[:, m * HT:(m + 1) * HT],
                            h3s[:, mi * HT:(mi + 1) * HT]), [("a", h_done)])
                        if kind == "e":
                            vedge(lambda e, s, m=m: e.tensor_mul(
                                h1s[:, m * HT:(m + 1) * HT],
                                h1s[:, m * HT:(m + 1) * HT], cbe[:]))
                for dc in range(4):
                    for k2 in range(NIC):
                        i = k2 % 2
                        aw = wst_a[i]
                        src2 = (w2T_h[ei, k2 * 128:(k2 + 1) * 128, dc * 512:(dc + 1) * 512]
                                if kind == "e" else
                                ws2T_h[ei * IEX + k2 * 128: ei * IEX + (k2 + 1) * 128,
                                       dc * 512:(dc + 1) * 512])
                        lw = p.dma("sync", wst[i][:, 0:512], src2,
                                   wait=[("a", aw)], sem="ld")
                        wst_a[i] = p.op(
                            "scalar", lambda e, s, k2=k2, i=i: e.activation(
                                w2_sb[:, k2 * 512:(k2 + 1) * 512], wst[i][:, 0:512],
                                AF.Identity),
                            wait=[("ld", lw), ("pe", p.n("pe"))], sem="a")
                    w2cast_a = p.n("a")
                    for tt in range(NT8):
                        pi = tt % 2
                        for k2 in range(NIC):
                            w_ = []
                            if k2 == 0:
                                w_ = [("a", w2cast_a), ("v", p.n("v"))]
                            p.op("tensor", lambda e, s, tt=tt, dc=dc, k2=k2, pi=pi: e.matmul(
                                yps[pi][:],
                                h1s[:, k2 * HT + tt * 128:k2 * HT + (tt + 1) * 128],
                                w2_sb[:, k2 * 512:(k2 + 1) * 512],
                                start=(k2 == 0), stop=(k2 == NIC - 1)),
                                wait=w_, sem=("pe" if k2 == NIC - 1 else None))
                        pw = p.n("pe")
                        vedge(lambda e, s, tt=tt, dc=dc, pi=pi: e.tensor_add(
                            y_acc[:, tt * D + dc * 512:tt * D + (dc + 1) * 512],
                            y_acc[:, tt * D + dc * 512:tt * D + (dc + 1) * 512],
                            yps[pi][:]), [("pe", pw)])

            yfin = p.n("v")
            for tt in range(NT8):
                row = tok0 + tt * 128
                lw = p.dma("sync", x1t[:], x1_dram[row:row + 128, :],
                           wait=[("v", yfin), ("st", p.n("st"))], sem="ld")
                vedge(lambda e, s, tt=tt: e.tensor_add(
                    yt[:], x1t[:], y_acc[:, tt * D:(tt + 1) * D]), [("ld", lw)])
                p.dma("sync", x2_h[row:row + 128, :], yt[:],
                      wait=[("v", p.n("v"))], sem="st")
        p.wait("sync", [("st", p.n("st"))])
        if standalone:
            p.finalize()
    return nc




HEAD = 16
SCL = float(128.0 ** -0.25)


def build_l1a(dbg=False, lnonly=0, mega=None):
    nc, p, dram, standalone = _mega_ctx(mega, "pd_")
    x_h = dram("x" if standalone else "x2", [TPC, D], F32, kind="ExternalInput")
    wq_h = dram("wq", [D, D], F32, kind="ExternalInput")
    wkv_h = dram("wkv", [D, 2 * D], F32, kind="ExternalInput")
    bq_h = dram("bq", [128, 16], F32, kind="ExternalInput")
    bk_h = dram("bk", [128, 16], F32, kind="ExternalInput")
    bv_h = dram("bv", [128, 16], F32, kind="ExternalInput")
    gq_h = dram("gq", [128, 16], F32, kind="ExternalInput")
    bgq_h = dram("bgq", [128, 16], F32, kind="ExternalInput")
    gkv_h = dram("gkv", [128, 16], F32, kind="ExternalInput")
    bgkv_h = dram("bgkv", [128, 16], F32, kind="ExternalInput")
    gh_h = dram("gh", [128, 1], F32, kind="ExternalInput")
    bh_h = dram("bh", [128, 1], F32, kind="ExternalInput")
    cost_h = dram("cost", [128, TPC], F32, kind="ExternalInput")
    sint_h = dram("sint", [128, TPC], F32, kind="ExternalInput")
    ident_h = dram("ident", [128, 128], BF16, kind="ExternalInput")
    identf_h = dram("identf", [128, 128], F32, kind="ExternalInput")
    t1_dram = dram("t1_dram", [D, TPC], BF16, kind="ExternalOutput")
    m_dram = dram("m_dram", [128, 1], F32, kind="ExternalOutput")
    qn_h = dram("qn", [D, TPC], BF16, kind="ExternalOutput")
    if standalone:
        kn_h = dram("kn", [D, TPC], BF16, kind="ExternalOutput")
        v_h = dram("v", [TPC, D], BF16, kind="ExternalOutput")
    else:
        kn_h = dram("kn_loc", [D, TPC], BF16)
        v_h = dram("v_loc", [TPC, D], BF16)
        knb_ag = dram("knb_ag", [4 * D, TPC], BF16)
        v_ag = dram("v_ag", [4 * TPC, D], BF16)
    dbg_h = dram("dbgo", [4, 128, TPC], F32, kind="ExternalOutput") if dbg else None

    for s_ in ("ld", "v", "a", "pe", "st", "cc"):
        p.sem(s_)

    with contextlib.ExitStack() as stack:
        E = stack.enter_context
        ident = E(nc.sbuf_tensor([128, 128], BF16))
        identf = E(nc.sbuf_tensor([128, 128], F32))
        xa_fm = E(nc.sbuf_tensor([128, NDC * TPC], BF16))
        mb = E(nc.sbuf_tensor([128, TPC], F32))
        rr = E(nc.sbuf_tensor([128, TPC], F32))
        ri = E(nc.sbuf_tensor([128, TPC], F32))
        bcols = {}
        for nm, h_ in (("bq", bq_h), ("bk", bk_h), ("bv", bv_h), ("gq", gq_h),
                       ("bgq", bgq_h), ("gkv", gkv_h), ("bgkv", bgkv_h)):
            bcols[nm] = E(nc.sbuf_tensor("c_" + nm, [128, 16], F32))
        gh = E(nc.sbuf_tensor([128, 1], F32))
        bh = E(nc.sbuf_tensor([128, 1], F32))
        sbias = E(nc.sbuf_tensor([128, 16], F32))
        ones_row_f = E(nc.sbuf_tensor([1, 128], F32))
        ones_col_f = E(nc.sbuf_tensor([128, 1], F32))
        mrowS = E(nc.sbuf_tensor([1, 128], F32))
        vstg = [E(nc.sbuf_tensor(f"vstg{i}", [128, 128], BF16)) for i in range(2)]
        wtile = [E(nc.sbuf_tensor(f"wt{i}", [128, 128], F32)) for i in range(2)]
        w_bf = E(nc.sbuf_tensor([128, NDC * 128], BF16))
        qk = E(nc.sbuf_tensor([128, TPC], F32))
        qs = E(nc.sbuf_tensor([128, TPC], F32))
        qrot = E(nc.sbuf_tensor([128, TPC], F32))
        ta = E(nc.sbuf_tensor([128, TPC], F32))
        tb = E(nc.sbuf_tensor([128, TPC], F32))
        sqh = E(nc.sbuf_tensor([128, TPC], F32))
        musum = E(nc.sbuf_tensor([1, TPC], F32))
        sqsum = E(nc.sbuf_tensor([1, TPC], F32))
        mu2row = E(nc.sbuf_tensor([1, 512], F32))
        bb1 = E(nc.sbuf_tensor([128, 512], F32))
        bb2 = E(nc.sbuf_tensor([128, 512], F32))
        outbf = E(nc.sbuf_tensor([128, TPC], BF16))
        kps = E(nc.psum_tensor([128, 2048], F32))
        rps = E(nc.psum_tensor([1, 512], F32))
        bps = E(nc.psum_tensor([128, 512], F32))
        ptr = E(nc.psum_tensor([128, 128], BF16))
        ptf = E(nc.psum_tensor([128, 128], F32))

        p.dma("sync", ident[:], ident_h[:], sem="ld")
        p.dma("sync", identf[:], identf_h[:], sem="ld")
        for nm, h_ in (("bq", bq_h), ("bk", bk_h), ("bv", bv_h), ("gq", gq_h),
                       ("bgq", bgq_h), ("gkv", gkv_h), ("bgkv", bgkv_h)):
            p.dma("sync", bcols[nm][:], h_[:], sem="ld")
        p.dma("sync", gh[:], gh_h[:], sem="ld")
        p.dma("sync", bh[:], bh_h[:], sem="ld")
        base_ld = p.n("ld")
        p.op("vector", lambda e, s: e.memset(ones_row_f[:], 1.0), sem="v")
        p.op("vector", lambda e, s: e.memset(ones_col_f[:], 1.0), sem="v")


        def vedge(fn, extra=None):
            w_ = [("v", p.n("v"))]
            if extra:
                w_ += extra
            p.op("vector", fn, wait=w_, sem="v")

        x_t = E(nc.sbuf_tensor([128, D], F32))
        mcol_o = E(nc.sbuf_tensor([128, 1], F32))
        mrow_o = E(nc.sbuf_tensor([1, 128], F32))
        r2i_o = E(nc.sbuf_tensor([128, 1], F32))
        with contextlib.ExitStack() as lns:
            El = lns.enter_context
            sq = El(nc.sbuf_tensor([128, D], F32))
            xn1 = El(nc.sbuf_tensor([128, D], F32))
            xn2 = El(nc.sbuf_tensor([128, D], BF16))
            r1 = El(nc.sbuf_tensor([128, 1], F32))
            r2 = El(nc.sbuf_tensor([128, 1], F32))
            inv = El(nc.sbuf_tensor([128, 1], F32))
            nbias = El(nc.sbuf_tensor([128, 1], F32))
            meanb = El(nc.sbuf_tensor([128, 1], F32))
            m2b = El(nc.sbuf_tensor([128, 1], F32))
            varb = El(nc.sbuf_tensor([128, 1], F32))
            lnvb = El(nc.sbuf_tensor([128, 1], F32))
            mcol = El(nc.sbuf_tensor([128, 1], F32))
            msq = El(nc.sbuf_tensor([128, 128], F32))
            scr2 = {nm: El(nc.sbuf_tensor("i_" + nm, [128, 1], F32))
                    for nm in ("r1", "r2", "inv", "nbias", "mean", "m2", "var", "lnv")}
            scr2["sq"] = El(nc.sbuf_tensor("i_sq", [128, D], F32))
            if dbg:
                dbg2_h = nc.dram_tensor("dbg2l1a", [8, 128, 1], F32, kind="ExternalOutput")
            scr = {"sq": sq, "r1": r1, "r2": r2, "inv": inv, "nbias": nbias,
                   "mean": meanb, "m2": m2b, "var": varb, "lnv": lnvb}
            for tt in range(NTT):
                row = tt * 128
                ldw = p.dma("sync", x_t[:], x_h[row:row + 128, :],
                            wait=[("v", p.n("v")), ("a", p.n("a"))], sem="ld")
                p.wait("vector", [("ld", ldw)])
                p.wait("scalar", [("ld", ldw)])
                emit_ln_tile(p, x_t[:], xn1[:], scr)
                emit_ln_tile(p, xn1[:], xn2[:], scr2, vsquare=True)
                vv = p.n("v")
                p.op("scalar", lambda e, s: e.activation(
                    mcol[:], scr2["r2"][:], AF.Sqrt, scale=float(D)),
                    wait=[("v", vv)], sem="a")
                aw = p.n("a")
                vedge(lambda e, s: e.tensor_copy(mcol_o[:], mcol[:]), [("a", aw)])
                vedge(lambda e, s: e.tensor_copy(r2i_o[:], scr2["r2"][:]))
                sm = p.dma("sync", m_dram[:], mcol[:], wait=[("a", aw)], sem="ld")
                p.wait("sync", [("ld", sm)])
                lm = p.dma("sync", mrowS[:],
                           m_dram[:].rearrange("p one -> one p"), sem="ld")
                p.wait("tensor", [("ld", lm)])
                vw = p.n("v")
                vedge(lambda e, s: e.tensor_copy(mrow_o[:], mrowS[:]), [("ld", lm)])
                p.op("tensor", lambda e, s: e.matmul(
                    ptf[:], ones_row_f[:], mrowS[:], start=True, stop=True),
                    wait=[("v", vw)], sem="pe")
                pw = p.n("pe")
                p.op("scalar", lambda e, s, row=row: e.activation(
                    mb[:, row:row + 128], ptf[:], AF.Identity),
                    wait=[("pe", pw), ("v", p.n("v")), ("a", p.n("a"))], sem="a")
                mb_done = p.n("a")
                for j in range(NDC):
                    vw = p.n("v")
                    p.op("tensor", lambda e, s, j=j: e.transpose(
                        ptr[:], xn2[:, j * 128:(j + 1) * 128], ident[:]),
                        wait=[("v", vw), ("a", mb_done)], sem="pe")
                    pw = p.n("pe")
                    p.op("vector", lambda e, s, tt=tt, j=j: e.tensor_copy(
                        xa_fm[:, j * TPC + tt * 128: j * TPC + (tt + 1) * 128], ptr[:]),
                        wait=[("pe", pw)], sem="v")
            if dbg:
                fv = p.n("v"); fa = p.n("a")
                p.wait("gpsimd", [("v", fv), ("a", fa)])
                for di, tq in enumerate((r1, r2, inv, meanb, varb, mcol,
                                         scr2["r1"], scr2["var"])):
                    p.dma("gpsimd", dbg2_h[di], tq[:], sem="st")
            if lnonly == 1:
                dbg3_h = nc.dram_tensor("dbg3", [2, 128, D], F32, kind="ExternalOutput")
                dbg5_h = nc.dram_tensor("dbg5", [4, 128, 128], F32, kind="ExternalOutput")
                fv = p.n("v"); fa = p.n("a")
                p.dma("sync", dbg3_h[0], x_t[:], wait=[("v", fv), ("a", fa)], sem="st")
                p.dma("sync", dbg3_h[1], xn1[:], sem="st")
                vedge(lambda e, s: e.tensor_copy(ta[:, 0:1], mcol[:]))
                vedge(lambda e, s: e.tensor_copy(ta[:, 1:2], scr2["r2"][:]))
                vedge(lambda e, s: e.tensor_copy(ta[:, 2:3], scr2["var"][:]))
                p.dma("sync", dbg5_h[0], ta[:, 0:128], wait=[("v", p.n("v"))], sem="st")
                p.dma("sync", dbg5_h[1, 0:1, :], mrowS[:], sem="st")
                p.dma("sync", dbg5_h[2], mb[:, TPC - 128:TPC], sem="st")
                p.wait("sync", [("st", p.n("st"))])
                p.finalize()
                return nc
        if lnonly == 6:
            dbg4_h = nc.dram_tensor("dbg4", [1, 128, TPC], F32, kind="ExternalOutput")
            p.dma("sync", dbg4_h[0], mb[:], wait=[("v", p.n("v")), ("a", p.n("a"))],
                  sem="st")
            p.wait("sync", [("st", p.n("st"))])
            p.finalize()
            return nc
        with contextlib.ExitStack() as cs_stack:
            cost = cs_stack.enter_context(nc.sbuf_tensor([128, TPC], F32))
            sint = cs_stack.enter_context(nc.sbuf_tensor([128, TPC], F32))
            lw1 = p.dma("sync", cost[:], cost_h[:], sem="ld")
            lw2 = p.dma("sync", sint[:], sint_h[:], sem="ld")
            vedge(lambda e, s: e.tensor_mul(rr[:], cost[:], mb[:]),
                  [("ld", lw2), ("a", p.n("a"))])
            vedge(lambda e, s: e.tensor_mul(ri[:], sint[:], mb[:]))
        prep_v = p.n("v")
        t1_v = p.n("v")
        if lnonly == 5:
            dbg4_h = nc.dram_tensor("dbg4", [3, 128, TPC], F32, kind="ExternalOutput")
            p.dma("sync", dbg4_h[0], mb[:], wait=[("v", t1_v), ("a", p.n("a"))], sem="st")
            vedge(lambda e, s: e.tensor_copy(x_t[:, 0:1], mcol_o[:]))
            vedge(lambda e, s: e.tensor_copy(x_t[:, 1:2], r2i_o[:]))
            vedge(lambda e, s: e.tensor_copy(x_t[0:1, 4:132], mrow_o[:]))
            p.dma("sync", dbg4_h[1], x_t[:], wait=[("v", p.n("v"))], sem="st")
            p.dma("sync", dbg4_h[2], xa_fm[:, 0:2 * TPC].bitcast(F32), sem="st")
        for dc in range(NDC):
            p.dma("sync", t1_dram[dc * 128:(dc + 1) * 128, :],
                  xa_fm[:, dc * TPC:(dc + 1) * TPC], wait=[("v", t1_v)], sem="st")
        t1_saved = p.n("st")
        if lnonly == 5:
            p.wait("sync", [("st", p.n("st"))])
            p.finalize()
            return nc
        if lnonly == 4:
            dbg4_h = nc.dram_tensor("dbg4", [1, 128, TPC], F32, kind="ExternalOutput")
            fv = p.n("v"); fa = p.n("a")
            p.dma("sync", dbg4_h[0], mb[:], wait=[("v", fv), ("a", fa),
                                                  ("st", t1_saved)], sem="st")
            p.wait("sync", [("st", p.n("st"))])
            p.finalize()
            return nc

        if lnonly == 2:
            dbg4_h = nc.dram_tensor("dbg4", [3, 128, TPC], F32, kind="ExternalOutput")
            fv = p.n("v"); fa = p.n("a")
            p.dma("sync", dbg4_h[0], mb[:], wait=[("v", fv), ("a", fa)], sem="st")
            vedge(lambda e, s: e.tensor_copy(rr[:], mb[:]), [("a", fa)])
            vedge(lambda e, s: e.tensor_copy(ri[:], rr[:]))
            p.dma("sync", dbg4_h[1], rr[:], wait=[("v", p.n("v"))], sem="st")
            p.dma("sync", dbg4_h[2], mb[:], sem="st")
            p.wait("sync", [("st", p.n("st"))])
            p.finalize()
            return nc

        def qkv_pass(is_q):
            gcol = bcols["gq"] if is_q else bcols["gkv"]
            bgcol = bcols["bgq"] if is_q else bcols["bgkv"]
            for dc in range(NDC):
                if is_q:
                    lw = p.dma("sync", xa_fm[:, dc * TPC:(dc + 1) * TPC],
                               t1_dram[dc * 128:(dc + 1) * 128, :],
                               wait=[("st", t1_saved), ("v", p.n("v")),
                                     ("pe", p.n("pe"))], sem="ld")
                    extra = [("ld", lw), ("a", p.n("a"))]
                else:
                    extra = [("a", p.n("a")), ("st", t1_saved)]
                vedge(lambda e, s, dc=dc, gcol=gcol, bgcol=bgcol: e.scalar_tensor_tensor(
                    xa_fm[:, dc * TPC:(dc + 1) * TPC], xa_fm[:, dc * TPC:(dc + 1) * TPC],
                    gcol[:, dc:dc + 1],
                    bgcol[:, dc:dc + 1].broadcast_to((128, TPC)), ALU.mult, ALU.add),
                    extra)
            xa_v = p.n("v")
            nmat = 1 if is_q else 2
            for h in range(HEAD):
                for mat in range(nmat):
                    for k in range(NDC):
                        i = k % 2
                        col0 = h * 128 if is_q else (mat * D + h * 128)
                        srcw = (wq_h if is_q else wkv_h)[k * 128:(k + 1) * 128,
                                                         col0:col0 + 128]
                        aw = p.n("a")
                        lw = p.dma("sync", wtile[i][:], srcw, wait=[("a", aw)], sem="ld")
                        p.op("scalar", lambda e, s, k=k, i=i: e.activation(
                            w_bf[:, k * 128:(k + 1) * 128], wtile[i][:], AF.Identity),
                            wait=[("ld", lw), ("pe", p.n("pe"))], sem="a")
                    cast_a = p.n("a")
                    for tc in range(4):
                        for k in range(NDC):
                            w_ = []
                            if k == 0:
                                w_ = [("a", cast_a), ("v", max(xa_v, p.n("v")))]
                            p.op("tensor", lambda e, s, tc=tc, k=k: e.matmul(
                                kps[:, tc * 512:(tc + 1) * 512],
                                w_bf[:, k * 128:(k + 1) * 128],
                                xa_fm[:, k * TPC + tc * 512:k * TPC + (tc + 1) * 512],
                                start=(k == 0), stop=(k == NDC - 1)),
                                wait=w_, sem=("pe" if k == NDC - 1 else None))
                    mm_pe = p.n("pe")
                    if (not is_q) and mat == 1:
                        p.op("scalar", lambda e, s, h=h: e.activation(
                            outbf[:], kps[:], AF.Identity,
                            bias=bcols["bv"][:, h:h + 1]),
                            wait=[("pe", mm_pe), ("st", p.n("st"))], sem="a")
                        aw = p.n("a")
                        p.wait("tensor", [("a", aw)])
                        for tt in range(NTT):
                            stg = vstg[tt % 2]
                            vw = p.n("v")
                            p.op("tensor", lambda e, s, tt=tt: e.transpose(
                                ptr[:], outbf[:, tt * 128:(tt + 1) * 128], ident[:]),
                                wait=[("v", vw)], sem="pe")
                            pw = p.n("pe")
                            vedge(lambda e, s, stg=stg: e.tensor_copy(stg[:], ptr[:]),
                                  [("pe", pw), ("st", max(0, p.n("st") - 16))])
                            p.dma("sync", v_h[tt * 128:(tt + 1) * 128, h * 128:(h + 1) * 128],
                                  stg[:], wait=[("v", p.n("v"))], sem="st")
                        continue
                    bcol = bcols["bq"] if is_q else bcols["bk"]
                    vedge(lambda e, s, bcol=bcol, h=h: e.tensor_scalar(
                        sbias[:, 0:1], bcol[:, h:h + 1], SCL, None, ALU.mult))
                    vv = p.n("v")
                    p.op("scalar", lambda e, s: e.activation(
                        qk[:], kps[:], AF.Identity, bias=sbias[:, 0:1], scale=SCL),
                        wait=[("pe", mm_pe), ("v", vv)], sem="a")
                    rot_a = p.n("a")
                    sw1 = p.dma("sync", qs[0:64, :], qk[64:128, :],
                                wait=[("a", rot_a), ("v", p.n("v"))], sem="ld")
                    sw2 = p.dma("sync", qs[64:128, :], qk[0:64, :], sem="ld")
                    vedge(lambda e, s: e.tensor_mul(ta[:], qk[:], rr[:]),
                          [("a", rot_a)])
                    vedge(lambda e, s: e.tensor_mul(tb[:], qs[:], ri[:]),
                          [("ld", sw2)])
                    vedge(lambda e, s: e.tensor_add(qrot[:], ta[:], tb[:]))
                    rot_v = p.n("v")
                    p.op("scalar", lambda e, s: e.activation(sqh[:], qrot[:], AF.Square),
                         wait=[("v", rot_v)], sem="a")
                    sq_a = p.n("a")
                    for which, srcb, dstrow in ((0, qrot, musum), (1, sqh, sqsum)):
                        for tc in range(4):
                            w_ = [("v", rot_v)] if which == 0 else [("a", sq_a)]
                            p.op("tensor", lambda e, s, srcb=srcb, tc=tc: e.matmul(
                                rps[:], ones_col_f[:],
                                srcb[:, tc * 512:(tc + 1) * 512],
                                start=True, stop=True),
                                wait=w_ + [("v", p.n("v"))], sem="pe")
                            pw = p.n("pe")
                            vedge(lambda e, s, dstrow=dstrow, tc=tc: e.tensor_copy(
                                dstrow[0:1, tc * 512:(tc + 1) * 512], rps[:]),
                                [("pe", pw)])
                    vedge(lambda e, s: e.tensor_scalar_mul(musum[:], musum[:], 1.0 / 128))
                    vedge(lambda e, s: e.tensor_scalar(
                        sqsum[:], sqsum[:], 1.0 / 128, EPS, ALU.mult, ALU.add))
                    for sc4 in range(4):
                        vedge(lambda e, s, sc4=sc4: e.tensor_mul(
                            mu2row[:], musum[0:1, sc4 * 512:(sc4 + 1) * 512],
                            musum[0:1, sc4 * 512:(sc4 + 1) * 512]))
                        vedge(lambda e, s, sc4=sc4: e.tensor_sub(
                            sqsum[0:1, sc4 * 512:(sc4 + 1) * 512],
                            sqsum[0:1, sc4 * 512:(sc4 + 1) * 512], mu2row[:]))
                    vv = p.n("v")
                    p.op("scalar", lambda e, s: e.activation(sqsum[:], sqsum[:], AF.Ln),
                         wait=[("v", vv)], sem="a")
                    p.op("scalar", lambda e, s: e.activation(sqsum[:], sqsum[:], AF.Exp,
                                                             scale=-0.5),
                         wait=[("a", p.n("a"))], sem="a")
                    st_a = p.n("a")
                    vedge(lambda e, s: e.tensor_mul(musum[:], musum[:], sqsum[:]),
                          [("a", st_a)])
                    vedge(lambda e, s: e.tensor_scalar_mul(musum[:], musum[:], -1.0))
                    row_v = p.n("v")
                    for tc in range(4):
                        for which, srcrow, dstb in ((0, sqsum, bb1), (1, musum, bb2)):
                            vw = p.n("v")
                            p.op("tensor", lambda e, s, srcrow=srcrow, tc=tc: e.matmul(
                                bps[:], ones_row_f[:], srcrow[0:1, tc * 512:(tc + 1) * 512],
                                start=True, stop=True),
                                wait=[("v", max(vw, row_v))], sem="pe")
                            pw = p.n("pe")
                            vedge(lambda e, s, dstb=dstb: e.tensor_copy(dstb[:], bps[:]),
                                  [("pe", pw)])
                        vedge(lambda e, s, tc=tc: e.tensor_mul(
                            qrot[:, tc * 512:(tc + 1) * 512],
                            qrot[:, tc * 512:(tc + 1) * 512], bb1[:]))
                        vedge(lambda e, s, tc=tc: e.tensor_add(
                            qrot[:, tc * 512:(tc + 1) * 512],
                            qrot[:, tc * 512:(tc + 1) * 512], bb2[:]))
                        vedge(lambda e, s, tc=tc: e.scalar_tensor_tensor(
                            outbf[:, tc * 512:(tc + 1) * 512],
                            qrot[:, tc * 512:(tc + 1) * 512], gh[:, 0:1],
                            bh[:, 0:1].broadcast_to((128, 512)), ALU.mult, ALU.add),
                            [("st", p.n("st"))] if tc == 0 else None)
                    ofin = p.n("v")
                    dst = qn_h if is_q else kn_h
                    p.dma("sync", dst[h * 128:(h + 1) * 128, :], outbf[:],
                          wait=[("v", ofin)], sem="st")

        qkv_pass(False)
        if lnonly == 3:
            dbg4_h = nc.dram_tensor("dbg4", [3, 128, TPC], F32, kind="ExternalOutput")
            fv = p.n("v"); fa = p.n("a"); fst = p.n("st")
            p.wait("gpsimd", [("v", fv), ("a", fa), ("st", fst)])
            p.dma("gpsimd", dbg4_h[0], mb[:], sem="st")
            p.dma("gpsimd", dbg4_h[1], rr[:], sem="st")
            p.dma("gpsimd", dbg4_h[2], qrot[:], sem="st")
            p.wait("gpsimd", [("st", p.n("st"))])
            p.wait("sync", [("st", p.n("st"))])
            p.finalize()
            return nc
        qkv_pass(True)
        if dbg:
            fin_v = p.n("v")
            fin_a = p.n("a")
            p.dma("sync", dbg_h[0], mb[:], wait=[("v", fin_v), ("a", fin_a)], sem="st")
            p.dma("sync", dbg_h[1], rr[:], sem="st")
            p.dma("sync", dbg_h[2], qk[:], sem="st")
            p.dma("sync", dbg_h[3], qrot[:], sem="st")
        if not standalone:
            p.wait("gpsimd", [("st", p.n("st"))])
            p.op("gpsimd", lambda e, s: e.collective_compute(
                "AllGather", ALU.bypass,
                replica_groups=[[0, 1, 2, 3], [4, 5, 6, 7]],
                ins=[kn_h[:]], outs=[knb_ag[:]]), sem="cc")
            p.op("gpsimd", lambda e, s: e.collective_compute(
                "AllGather", ALU.bypass,
                replica_groups=[[0, 1, 2, 3], [4, 5, 6, 7]],
                ins=[v_h[:]], outs=[v_ag[:]]), sem="cc")
        p.wait("sync", [("st", p.n("st"))])
        if standalone:
            p.finalize()
    return nc



LB = 8192


def build_l1b(upto="full", mega=None):
    nc, p, dram, standalone = _mega_ctx(mega, "pe_")
    qn_h = dram("qn", [D, TPC], BF16, kind="ExternalInput")
    if standalone:
        knb_h = dram("knb", [D, LB], BF16, kind="ExternalInput")
        vb_h = dram("vb", [LB, D], BF16, kind="ExternalInput")
    else:
        knb_ag = dram("knb_ag", [4 * D, TPC], BF16)
        vb_h = dram("v_ag", [4 * TPC, D], BF16)
    wo_h = dram("wo", [D, D], F32, kind="ExternalInput")
    bo_h = dram("bo", [1, D], F32, kind="ExternalInput")
    x2_h = dram("x2", [TPC, D], F32, kind="ExternalInput")
    xb_dram = dram("xb_dram", [HEAD, NTT, 128, 1], BF16, kind="ExternalOutput")
    rc_dram = dram("rc_dram", [HEAD, NTT, 128, 1], F32, kind="ExternalOutput")
    x3_h = dram("x3", [TPC, D], F32, kind="ExternalOutput")

    S128 = float(1.0 / np.sqrt(128.0))
    SQ128 = float(np.sqrt(128.0))

    for s_ in ("ld", "v", "a", "pe", "st", "kld", "x2l", "cc"):
        p.sem(s_)

    with contextlib.ExitStack() as stack:
        E = stack.enter_context
        av_fm = E(nc.sbuf_tensor([128, HEAD * TPC], BF16))
        ones_row_f = E(nc.sbuf_tensor([1, 128], F32))
        ones_row_b = E(nc.sbuf_tensor([1, 128], BF16))
        borow = E(nc.sbuf_tensor([1, D], F32))
        bpsC = E(nc.psum_tensor([128, 512], F32))

        def vedge(fn, extra=None):
            w_ = [("v", p.n("v"))]
            if extra:
                w_ += extra
            p.op("vector", fn, wait=w_, sem="v")

        p.op("vector", lambda e, s: e.memset(ones_row_f[:], 1.0))
        p.op("vector", lambda e, s: e.memset(ones_row_b[:], 1.0), sem="v")
        if upto == "h1":
            p.op("vector", lambda e, s: e.memset(av_fm[:], 0.0), sem="v")
        ones_v = p.n("v")
        lb = p.dma("sync", borow[:], bo_h[:], sem="ld")

        with contextlib.ExitStack() as qs_stack:
            Eq = qs_stack.enter_context
            qn_sb = Eq(nc.sbuf_tensor([128, HEAD * TPC], BF16))
            psA = [Eq(nc.psum_tensor(f"psA{i}", [128, 1024], F32)) for i in range(2)]
            avps = Eq(nc.psum_tensor("avps", [128, 1024], F32))
            knh = Eq(nc.sbuf_tensor([128, LB], BF16))
            vh = Eq(nc.sbuf_tensor([128, LB], BF16))
            Ebuf = [Eq(nc.sbuf_tensor(f"Eb{i}", [128, 1024], BF16)) for i in range(2)]
            trash = Eq(nc.sbuf_tensor([128, 1024], F32))
            nm = Eq(nc.sbuf_tensor([128, 8], F32))
            nms = Eq(nc.sbuf_tensor([128, 8], F32))
            sc = Eq(nc.sbuf_tensor([128, 8], F32))
            nmg = Eq(nc.sbuf_tensor([128, 1], F32))
            earg = Eq(nc.sbuf_tensor([128, 8], F32))
            dcol = Eq(nc.sbuf_tensor([128, 1], F32))
            xcol = Eq(nc.sbuf_tensor([128, 1], F32))
            xb16 = Eq(nc.sbuf_tensor([128, 1], BF16))
            xb32 = Eq(nc.sbuf_tensor([128, 1], F32))
            ccol = Eq(nc.sbuf_tensor([128, 1], F32))
            xrow = Eq(nc.sbuf_tensor([1, TPC], BF16))
            rcrow = Eq(nc.sbuf_tensor([1, TPC], F32))
            corrb = Eq(nc.sbuf_tensor([128, 512], F32))
            red2 = Eq(nc.sbuf_tensor([128, 2], F32))
            sc2 = Eq(nc.sbuf_tensor([128, 2], F32))
            for h in range(HEAD):
                lq = p.dma("sync", qn_sb[:, h * TPC:(h + 1) * TPC],
                           qn_h[h * 128:(h + 1) * 128, :], sem="ld")
            qn_ld = p.n("ld")

            for h in (range(1) if upto in ("passA", "h1") else range(HEAD)):
                if standalone:
                    lk = p.dma("sync", knh[:], knb_h[h * 128:(h + 1) * 128, :],
                               wait=[("pe", p.n("pe"))], sem="kld")
                else:
                    ccn = p.n("cc")
                    for j in range(4):
                        lk = p.dma(
                            "sync", knh[:, j * TPC:(j + 1) * TPC],
                            knb_ag[j * D + h * 128:j * D + (h + 1) * 128, :],
                            wait=([("pe", p.n("pe")), ("cc", ccn)]
                                  if j == 0 else None), sem="kld")
                lv = p.dma("sync", vh[:].rearrange("p (kt c) -> p kt c", c=128),
                           vb_h[:, h * 128:(h + 1) * 128].rearrange(
                               "(kt p) c -> p kt c", p=128),
                           sem="kld")
                kv_ld = p.n("kld")
                for qt in range(NTT):
                    for kc in range(8):
                        pi = kc % 2
                        for c2 in range(2):
                            w_ = [("kld", kv_ld), ("ld", qn_ld)]
                            if c2 == 0:
                                w_.append(("a", p.n("a")))
                                w_.append(("v", p.n("v")))
                            p.op("tensor", lambda e, s, h=h, qt=qt, kc=kc, c2=c2,
                                 pi=pi: e.matmul(
                                psA[pi][:, c2 * 512:(c2 + 1) * 512],
                                qn_sb[:, h * TPC + qt * 128:h * TPC + (qt + 1) * 128],
                                knh[:, kc * 1024 + c2 * 512:kc * 1024 + (c2 + 1) * 512],
                                start=True, stop=True),
                                wait=w_, sem=("pe" if c2 == 1 else None))
                        pw = p.n("pe")
                        p.op("vector", lambda e, s, kc=kc, pi=pi: e.tensor_reduce(
                            red2[:, 0:1], psA[pi][:, 0:512], AX.X, ALU.max, negate=True),
                            wait=[("pe", pw), ("v", p.n("v"))], sem="v")
                        vedge(lambda e, s, kc=kc, pi=pi: e.tensor_reduce(
                            red2[:, 1:2], psA[pi][:, 512:1024], AX.X, ALU.max,
                            negate=True))
                        vedge(lambda e, s, kc=kc: e.tensor_reduce(
                            nm[:, kc:kc + 1], red2[:], AX.X, ALU.min))
                        vedge(lambda e, s, kc=kc: e.tensor_scalar(
                            nms[:, kc:kc + 1], nm[:, kc:kc + 1], S128, None, ALU.mult))
                        vv = p.n("v")
                        p.op("scalar", lambda e, s, kc=kc, pi=pi: e.activation(
                            trash[:, 0:512], psA[pi][:, 0:512], AF.Exp,
                            bias=nms[:, kc:kc + 1],
                            scale=S128, accum_out=sc2[:, 0:1]),
                            wait=[("v", vv), ("a", p.n("a"))], sem="a")
                        p.op("scalar", lambda e, s, kc=kc, pi=pi: e.activation(
                            trash[:, 512:1024], psA[pi][:, 512:1024], AF.Exp,
                            bias=nms[:, kc:kc + 1],
                            scale=S128, accum_out=sc2[:, 1:2]),
                            wait=[("a", p.n("a"))], sem="a")
                        aw2 = p.n("a")
                        vedge(lambda e, s, kc=kc: e.tensor_reduce(
                            sc[:, kc:kc + 1], sc2[:], AX.X, ALU.add), [("a", aw2)])
                    a_fin = p.n("a")
                    vedge(lambda e, s: e.tensor_reduce(
                        nmg[:], nm[:], AX.X, ALU.min), [("a", a_fin)])
                    vedge(lambda e, s: e.scalar_tensor_tensor(
                        earg[:], nm[:], -S128, nmg[:, 0:1].broadcast_to((128, 8)),
                        ALU.mult, ALU.bypass) if False else
                        e.tensor_scalar(earg[:], nm[:], -1.0, None, ALU.mult))
                    vedge(lambda e, s: e.scalar_tensor_tensor(
                        earg[:], earg[:], 1.0, nmg[:, 0:1].broadcast_to((128, 8)),
                        ALU.mult, ALU.add))
                    vedge(lambda e, s: e.tensor_scalar(
                        earg[:], earg[:], S128, None, ALU.mult))
                    vv = p.n("v")
                    p.op("scalar", lambda e, s: e.activation(earg[:], earg[:], AF.Exp),
                         wait=[("v", vv)], sem="a")
                    aw = p.n("a")
                    vedge(lambda e, s: e.tensor_mul(earg[:], earg[:], sc[:]),
                          [("a", aw)])
                    vedge(lambda e, s: e.tensor_reduce(dcol[:], earg[:], AX.X, ALU.add))
                    vv = p.n("v")
                    p.op("scalar", lambda e, s: e.activation(xcol[:], dcol[:], AF.Ln),
                         wait=[("v", vv)], sem="a")
                    aw = p.n("a")
                    vedge(lambda e, s: e.tensor_scalar(
                        xcol[:], xcol[:], -SQ128, None, ALU.mult), [("a", aw)])
                    vedge(lambda e, s: e.tensor_add(xcol[:], xcol[:], nmg[:]))
                    vedge(lambda e, s: e.tensor_copy(xb16[:], xcol[:]),
                          [("st", p.n("st"))])
                    vedge(lambda e, s: e.tensor_copy(xb32[:], xb16[:]))
                    vedge(lambda e, s: e.tensor_sub(ccol[:], xb32[:], nmg[:]),
                          [("st", p.n("st"))])
                    vv = p.n("v")
                    p.op("scalar", lambda e, s: e.activation(
                        ccol[:], ccol[:], AF.Exp, scale=S128),
                        wait=[("v", vv)], sem="a")
                    aw = p.n("a")
                    vedge(lambda e, s: e.tensor_mul(ccol[:], ccol[:], dcol[:]),
                          [("a", aw)])
                    vedge(lambda e, s: e.reciprocal(ccol[:], ccol[:]))
                    vfin = p.n("v")
                    p.dma("sync", xb_dram[h, qt], xb16[:],
                          wait=[("v", vfin)], sem="st")
                    p.dma("sync", rc_dram[h, qt], ccol[:], sem="st")
                if upto == "passA":
                    continue
                stw = p.n("st")
                p.wait("sync", [("st", stw)])
                lx = p.dma("sync", xrow[:],
                           xb_dram[h].rearrange("qt p one -> (one) (qt p one)")
                           if False else
                           xb_dram[h].rearrange("qt p one -> one (qt p)"), sem="ld")
                lr = p.dma("sync", rcrow[:],
                           rc_dram[h].rearrange("qt p one -> one (qt p)"), sem="ld")
                for qh in range(2):
                    for kt in range(64):
                        pi = kt % 2
                        for c2 in range(2):
                            w_ = [("kld", kv_ld), ("ld", p.n("ld"))]
                            if c2 == 0:
                                w_.append(("a", p.n("a")))
                            p.op("tensor", lambda e, s, h=h, qh=qh, kt=kt, c2=c2,
                                 pi=pi: e.matmul(
                                psA[pi][:, c2 * 512:(c2 + 1) * 512],
                                knh[:, kt * 128:(kt + 1) * 128],
                                qn_sb[:, h * TPC + qh * 1024 + c2 * 512:
                                      h * TPC + qh * 1024 + (c2 + 1) * 512],
                                start=True, stop=False), wait=w_)
                            p.op("tensor", lambda e, s, qh=qh, kt=kt, c2=c2,
                                 pi=pi: e.matmul(
                                psA[pi][:, c2 * 512:(c2 + 1) * 512],
                                ones_row_b[:],
                                xrow[0:1, qh * 1024 + c2 * 512:
                                     qh * 1024 + (c2 + 1) * 512],
                                start=False, stop=True),
                                sem=("pe" if c2 == 1 else None))
                        pw = p.n("pe")
                        p.op("scalar", lambda e, s, kt=kt, pi=pi: e.activation(
                            Ebuf[pi][:], psA[pi][:], AF.Exp, scale=S128),
                            wait=[("pe", pw), ("a", p.n("a")),
                                  ("pe", p.n("pe"))], sem="a")
                        a_e = p.n("a")
                        for c2 in range(2):
                            w_av = [("a", a_e)] if c2 == 0 else []
                            if kt == 0 and c2 == 0:
                                w_av.append(("v", p.n("v")))
                            p.op("tensor", lambda e, s, kt=kt, c2=c2, pi=pi: e.matmul(
                                avps[:, c2 * 512:(c2 + 1) * 512],
                                vh[:, kt * 128:(kt + 1) * 128],
                                Ebuf[pi][:, c2 * 512:(c2 + 1) * 512],
                                start=(kt == 0), stop=(kt == 63)),
                                wait=w_av,
                                sem=("pe" if (kt == 63 and c2 == 1) else None))
                    pw = p.n("pe")
                    for c2 in range(2):
                        vw = p.n("v")
                        p.op("tensor", lambda e, s, qh=qh, c2=c2: e.matmul(
                            bpsC[:], ones_row_f[:],
                            rcrow[0:1, qh * 1024 + c2 * 512:qh * 1024 + (c2 + 1) * 512],
                            start=True, stop=True),
                            wait=[("ld", p.n("ld")), ("v", vw)], sem="pe")
                        pw2 = p.n("pe")
                        vedge(lambda e, s: e.tensor_copy(corrb[:], bpsC[:]),
                              [("pe", pw2)])
                        vedge(lambda e, s, h=h, qh=qh, c2=c2: e.tensor_mul(
                            av_fm[:, h * TPC + qh * 1024 + c2 * 512:
                                  h * TPC + qh * 1024 + (c2 + 1) * 512],
                            avps[:, c2 * 512:(c2 + 1) * 512].bitcast(F32)
                            if False else avps[:, c2 * 512:(c2 + 1) * 512],
                            corrb[:]), [("pe", pw)])
        if upto == "passA":
            p.wait("sync", [("st", p.n("st"))])
            p.finalize()
            return nc
        with contextlib.ExitStack() as wo_stack:
            Ew = wo_stack.enter_context
            wo_sb = Ew(nc.sbuf_tensor([128, NDC * D], BF16))
            wof = [Ew(nc.sbuf_tensor(f"wof{i}", [128, 512], F32)) for i in range(2)]
            wof_a = [p.n("a"), p.n("a")]
            bo_b = Ew(nc.sbuf_tensor([128, D], F32))
            x2t = Ew(nc.sbuf_tensor([128, D], F32))
            ot = Ew(nc.sbuf_tensor([128, D], F32))
            psO = [Ew(nc.psum_tensor(f"psO{i}", [128, 512], F32)) for i in range(2)]
            for c4 in range(4):
                vw = p.n("v")
                p.op("tensor", lambda e, s, c4=c4: e.matmul(
                    bpsC[:], ones_row_f[:], borow[0:1, c4 * 512:(c4 + 1) * 512],
                    start=True, stop=True),
                    wait=[("ld", p.n("ld")), ("v", max(ones_v, vw))], sem="pe")
                pw = p.n("pe")
                vedge(lambda e, s, c4=c4: e.tensor_copy(
                    bo_b[:, c4 * 512:(c4 + 1) * 512], bpsC[:]), [("pe", pw)])
            for k in range(NDC):
                for part in range(4):
                    i = (k * 4 + part) % 2
                    aw = wof_a[i]
                    lw = p.dma("sync", wof[i][:],
                               wo_h[k * 128:(k + 1) * 128, part * 512:(part + 1) * 512],
                               wait=[("a", aw)], sem="ld")
                    wof_a[i] = p.op(
                        "scalar", lambda e, s, k=k, part=part, i=i: e.activation(
                            wo_sb[:, k * D + part * 512:k * D + (part + 1) * 512],
                            wof[i][:], AF.Identity), wait=[("ld", lw)], sem="a")
            wo_cast = p.n("a")
            for tt in range(NTT):
                lx2 = p.dma("sync", x2t[:], x2_h[tt * 128:(tt + 1) * 128, :],
                            wait=[("st", p.n("st")), ("v", p.n("v"))], sem="x2l")
                vedge(lambda e, s: e.tensor_add(x2t[:], x2t[:], bo_b[:]),
                      [("x2l", lx2)])
                x2_v = p.n("v")
                for oc in range(4):
                    pi = (tt * 4 + oc) % 2
                    for k in range(NDC):
                        w_ = []
                        if k == 0:
                            w_ = [("a", wo_cast), ("v", p.n("v"))]
                        p.op("tensor", lambda e, s, tt=tt, oc=oc, k=k, pi=pi: e.matmul(
                            psO[pi][:],
                            av_fm[:, k * TPC + tt * 128:k * TPC + (tt + 1) * 128],
                            wo_sb[:, k * D + oc * 512:k * D + (oc + 1) * 512],
                            start=(k == 0), stop=(k == NDC - 1)),
                            wait=w_, sem=("pe" if k == NDC - 1 else None))
                    pw = p.n("pe")
                    vedge(lambda e, s, tt=tt, oc=oc, pi=pi: e.scalar_tensor_tensor(
                        ot[:, oc * 512:(oc + 1) * 512], psO[pi][:], 1.0,
                        x2t[:, oc * 512:(oc + 1) * 512], ALU.mult, ALU.add),
                        [("pe", pw), ("v", x2_v)])
                p.dma("sync", x3_h[tt * 128:(tt + 1) * 128, :], ot[:],
                      wait=[("v", p.n("v"))], sem="st")
        p.wait("sync", [("st", p.n("st"))])
        if standalone:
            p.finalize()
    return nc

_IDENT_BF16 = np.eye(128, dtype=bf16)


def run_l1c(x3_flat, w_fc, w_proj):
    nc = build_l1c()
    wfcT = np.ascontiguousarray(w_fc.T)
    wprojT = np.ascontiguousarray(w_proj.T)
    in_maps = []
    for c in range(NCORES):
        in_maps.append({
            "x": np.ascontiguousarray(x3_flat[c * TPC:(c + 1) * TPC]),
            "wfcT": wfcT, "wprojT": wprojT, "ident": _IDENT_BF16,
        })
    res = run_bass_kernel_spmd(nc, in_maps, core_ids=list(range(NCORES)))
    return np.concatenate([res.results[c]["out"] for c in range(NCORES)], axis=0)


_IDENT_F32 = np.eye(128, dtype=np.float32)
_ONES_ROW = np.ones((1, 128), dtype=bf16)
_SELB = np.zeros((32, 32 * 128), dtype=bf16)
for _r in range(32):
    _SELB[_r, _r * 128:(_r + 1) * 128] = 1.0
_SEL16 = np.zeros((16, 16 * 128), dtype=bf16)
for _r in range(16):
    _SEL16[_r, _r * 128:(_r + 1) * 128] = 1.0
_HPERM = np.array([2 * j if j < 64 else 2 * (j - 64) + 1 for j in range(128)])
_PERM = np.zeros(D, dtype=np.int64)
for _h in range(HEAD):
    for _j in range(128):
        _PERM[_h * 128 + _j] = _h * 128 + (2 * _j if _j < 64 else 2 * (_j - 64) + 1)


def _c16(v):
    return np.ascontiguousarray(np.asarray(v).reshape(16, 128).T)


def _run(nc, in_maps):
    return run_bass_kernel_spmd(nc, in_maps, core_ids=list(range(NCORES)))


def kernel(**inputs):
    nc = bass.Bass("TRN2", target_bir_lowering=False)
    p = Prog(nc)
    reg = {}
    mega = (nc, p, reg)
    build_l0a(mega=mega)
    build_l0b(mega=mega)
    build_l0c(mega=mega)
    build_l1a(mega=mega)
    build_l1b(mega=mega)
    build_l1c(mega=mega)
    p.finalize()

    x0 = np.ascontiguousarray(inputs["L0_attn_x"].reshape(-1, D))
    b_dbc = inputs["L0_attn_b_dbc"]
    wq = np.ascontiguousarray(inputs["L1_attn_Wq"][:, _PERM])
    wkv = np.ascontiguousarray(np.concatenate(
        [inputs["L1_attn_Wkv"][:, :D][:, _PERM], inputs["L1_attn_Wkv"][:, D:]], axis=1))
    HD2 = 64
    f = (8000.0 / 200.0) ** np.linspace(0.0, 1.0, HD2).astype(np.float32) * 200.0 / 1000.0
    freqs = (200.0 * f / 1000.0).astype(np.float32)
    shared = {
        "wdbcT": np.ascontiguousarray(inputs["L0_attn_W_dbc"].T),
        "bdbc_lo": np.ascontiguousarray(b_dbc[:128].reshape(128, 1)),
        "bdbc_bc": np.ascontiguousarray(b_dbc[128:].reshape(32, 1)),
        "wdtT": np.ascontiguousarray(inputs["L0_attn_W_dt"].T),
        "bdt": _c16(inputs["L0_attn_b_dt"]),
        "dvec": np.ascontiguousarray(inputs["L0_attn_D"].reshape(1, D)),
        "ident": _IDENT_BF16, "ones_row": _ONES_ROW,
        "alog": inputs["L0_attn_A_log"], "selb": _SELB,
        "gwT": np.ascontiguousarray(inputs["L0_ffn_gate_w"].T),
        "w1T": np.ascontiguousarray(inputs["L0_ffn_W1"].transpose(0, 2, 1)),
        "w3T": np.ascontiguousarray(inputs["L0_ffn_W3"].transpose(0, 2, 1)),
        "w2T": np.ascontiguousarray(inputs["L0_ffn_W2"].transpose(0, 2, 1)),
        "ws1T": np.ascontiguousarray(inputs["L0_ffn_Ws1"].T),
        "ws3T": np.ascontiguousarray(inputs["L0_ffn_Ws3"].T),
        "ws2T": np.ascontiguousarray(inputs["L0_ffn_Ws2"].T),
        "identf": _IDENT_F32, "sel16": _SEL16,
        "wq": wq, "wkv": wkv,
        "bq": _c16(inputs["L1_attn_bq"][_PERM]),
        "bk": _c16(inputs["L1_attn_bkv"][:D][_PERM]),
        "bv": _c16(inputs["L1_attn_bkv"][D:]),
        "gq": _c16(inputs["L1_attn_ln_q_g"]),
        "bgq": _c16(inputs["L1_attn_ln_q_b"]),
        "gkv": _c16(inputs["L1_attn_ln_kv_g"]),
        "bgkv": _c16(inputs["L1_attn_ln_kv_b"]),
        "gh": np.ascontiguousarray(inputs["L1_attn_ln_h_g"][_HPERM].reshape(128, 1)),
        "bh": np.ascontiguousarray(inputs["L1_attn_ln_h_b"][_HPERM].reshape(128, 1)),
        "wo": inputs["L1_attn_Wo"],
        "bo": np.ascontiguousarray(inputs["L1_attn_bo"].reshape(1, D)),
        "wfcT": np.ascontiguousarray(inputs["L1_ffn_w_fc"].T),
        "wprojT": np.ascontiguousarray(inputs["L1_ffn_w_proj"].T),
    }
    maps = []
    for c in range(NCORES):
        m = np.zeros(8, np.float32)
        for r in range(8):
            if r // 4 == c // 4 and r % 4 < c % 4:
                m[r] = 1.0
        tpos = np.arange((c % 4) * TPC, (c % 4) * TPC + TPC, dtype=np.float32)
        ang = np.outer(tpos, freqs)
        d = dict(shared)
        d["x"] = np.ascontiguousarray(x0[c * TPC:(c + 1) * TPC])
        d["mask"] = np.ascontiguousarray(np.repeat(m.reshape(1, 8), 128, 0))
        d["imask"] = np.ascontiguousarray(np.repeat((1 - m).reshape(1, 8), 128, 0))
        d["cost"] = np.ascontiguousarray(np.concatenate(
            [np.cos(ang).T, np.cos(ang).T], 0).astype(np.float32))
        d["sint"] = np.ascontiguousarray(np.concatenate(
            [-np.sin(ang).T, np.sin(ang).T], 0).astype(np.float32))
        maps.append(d)
    res = run_bass_kernel_spmd(nc, maps, core_ids=list(range(NCORES)))
    out = np.concatenate(
        [np.asarray(res.results[c]["out"]) for c in range(NCORES)], axis=0)
    return out.reshape(B, L, D)
